# version 08
# baseline (speedup 1.0000x reference)
import math
import numpy as np
import concourse.bass as bass
import concourse.mybir as mybir
from concourse.bass_utils import run_bass_kernel_spmd
from contextlib import ExitStack

F32 = mybir.dt.float32
BF16 = mybir.dt.bfloat16
AF = mybir.ActivationFunctionType
ALU = mybir.AluOpType
AX = mybir.AxisListType

B, L, D = 8, 2048, 2048
H, HD = 16, 128
RQ, RKV = 1536, 512
NE, DFF = 32, 768
SCALE = 1.0 / math.sqrt(HD)
_BASE = 2.0 ** (-2.0 ** (-(math.log2(H) - 3)))
SLOPES = [_BASE ** (i + 1) for i in range(H)]
NKC = D // 128
NTB = L // 128
EPS = 1e-5
M_SEQ = [0, 6, 1, 7, 2, 8, 3, 9, 4, 10, 5, 11]
DBG = False


def gsize(h):
    return 128 if h == 0 else (256 if h <= 2 else 512)


class KB:
    def __init__(self):
        self.nc = bass.Bass("TRN2", target_bir_lowering=False)
        self.sems = {}
        self.hwm = {}

    def sem(self, name):
        self.sems[name] = [self.nc.alloc_semaphore(name), 0]

    def inc(self, instr, name, n=1):
        rec = self.sems[name]
        instr.then_inc(rec[0], n)
        rec[1] += n
        return (name, rec[1])

    def cnt(self, name):
        return (name, self.sems[name][1])

    def wait(self, eng, name, val):
        if val is None or val <= 0:
            return
        rec = self.sems[name]
        assert val <= rec[1], (name, val, rec[1])
        key = (id(eng), name)
        if self.hwm.get(key, 0) >= val:
            return
        eng.wait_ge(rec[0], val)
        self.hwm[key] = val

    def wv(self, eng, evt):
        if evt is not None:
            self.wait(eng, evt[0], evt[1])

    def fence(self, eng, instr, sem):
        e = self.inc(instr, sem)
        self.wait(eng, e[0], e[1])
        return e


def build():
    kb = KB()
    nc = kb.nc
    PE, ACT, DVE, PL, SP = nc.tensor, nc.scalar, nc.vector, nc.gpsimd, nc.sync

    x_h = nc.dram_tensor("x", [L, D], F32, kind="ExternalInput")
    wqdT = nc.dram_tensor("wqdT", [12, 128, 16, 128], F32, kind="ExternalInput")
    wquT = nc.dram_tensor("wquT", [16, 128, 12, 128], F32, kind="ExternalInput")
    wkvdT = nc.dram_tensor("wkvdT", [4, 128, 16, 128], F32, kind="ExternalInput")
    kupT_h = nc.dram_tensor("kupT", [4, 128, 2048], F32, kind="ExternalInput")
    vupT_h = nc.dram_tensor("vupT", [4, 128, 2048], F32, kind="ExternalInput")
    woT_h = nc.dram_tensor("woT", [16, 128, 2048], F32, kind="ExternalInput")
    gwT_h = nc.dram_tensor("gwT", [128, 512], F32, kind="ExternalInput")
    sim_h = nc.dram_tensor("sim", [32, 32], F32, kind="ExternalInput")
    wg_h = nc.dram_tensor("wg", [32, 12, 128, 16, 128], F32, kind="ExternalInput")
    wd_h = nc.dram_tensor("wd", [32, 6, 128, 2048], F32, kind="ExternalInput")
    x1s = nc.dram_tensor("x1s", [L, D], F32, kind="Internal")
    ots = nc.dram_tensor("ots", [D, L], BF16, kind="Internal")
    out_h = nc.dram_tensor("out", [L, D], F32, kind="ExternalOutput")
    if DBG:
        dbg_x1 = nc.dram_tensor("dbg_x1", [L, D], F32, kind="ExternalOutput")
        dbg_ot = nc.dram_tensor("dbg_ot", [D, L], F32, kind="ExternalOutput")
        dbg_cw = nc.dram_tensor("dbg_cw", [L, 32], F32, kind="ExternalOutput")
        dbg_xn = nc.dram_tensor("dbg_xn", [L, D], F32, kind="ExternalOutput")
        dbg_pr = nc.dram_tensor("dbg_pr", [L, 32], F32, kind="ExternalOutput")
        dbg_s = nc.dram_tensor("dbg_s", [16, L], F32, kind="ExternalOutput")
        dbg_xn1 = nc.dram_tensor("dbg_xn1", [2, D, 1024], F32, kind="ExternalOutput")
        dbg_h = nc.dram_tensor("dbg_h", [768, 1024], F32, kind="ExternalOutput")
        dbg_y0 = nc.dram_tensor("dbg_y0", [128, 2048], F32, kind="ExternalOutput")
        dbg_wv = nc.dram_tensor("dbg_wv", [L, 8], F32, kind="ExternalOutput")

    for s in ["pe", "act", "dve", "pl", "dSim", "dGw", "dX0", "dX1",
              "dA1_0", "dA1_1", "dKup", "dQv0", "dQv1", "dOts0", "dOts1",
              "dWo0", "dWo1", "dWo2", "dWo3",
              "dOtb0", "dOtb1", "dX1s0", "dX1s1", "dXb0", "dXb1", "dYl",
              "dWg0", "dWg1", "dWd0", "dWd1", "dOut", "dDbg"]:
        kb.sem(s)
    for rec in kb.sems.values():
        nc.gpsimd.sem_clear(rec[0])
    nc.all_engine_barrier()

    top = ExitStack()

    def sb(name, shape, dtype, ctx):
        return ctx.enter_context(nc.sbuf_tensor(name, shape, dtype))

    ident_bf = sb("ident_bf", [128, 128], BF16, top)
    identf = sb("identf", [128, 128], F32, top)
    trimask = sb("trimask", [128, 128], BF16, top)
    ones_col = sb("ones_col", [128, 1], BF16, top)
    ones_row = sb("ones_row", [65, 128], BF16, top)
    iota_p = sb("iota_p", [128, 1], F32, top)
    reviota = sb("reviota", [128, 32], F32, top)
    sim_sb = sb("sim_sb", [32, 32], F32, top)
    gwT_sb = sb("gwT_sb", [128, 512], BF16, top)
    sq_scr = sb("sq_scr", [128, 2048], F32, top)
    con_eps = sb("con_eps", [128, 1], F32, top)
    con_scl = sb("con_scl", [128, 1], F32, top)
    con_half = sb("con_half", [128, 1], F32, top)
    stat = [sb(f"stat{i}", [128, 8], F32, top) for i in range(2)]

    bias_keys = {}
    for h in range(H):
        S = gsize(h)
        for g in range(L // S):
            c = g * S + (S - 1) / 2.0
            for kbk in range((g + 1) * S // 128):
                bias_keys.setdefault((h, kbk * 128 - c), len(bias_keys))
    NBC = len(bias_keys)
    biascols = sb("biascols", [128, NBC], F32, top)

    pctx = ExitStack()
    p = [pctx.enter_context(nc.psum_tensor(f"p{i}", [128, 512], F32))
         for i in range(6)]
    pt = [pctx.enter_context(nc.psum_tensor(f"pt{i}", [128, 128], BF16))
          for i in range(2)]
    bank_free = [None] * 6

    PL.memset(ident_bf[:], 1.0)
    PL.affine_select(ident_bf[:], ident_bf[:], [[1, 128]], ALU.is_ge, 0.0,
                     base=0, channel_multiplier=-1)
    PL.affine_select(ident_bf[:], ident_bf[:], [[-1, 128]], ALU.is_ge, 0.0,
                     base=0, channel_multiplier=1)
    PL.memset(identf[:], 1.0)
    PL.affine_select(identf[:], identf[:], [[1, 128]], ALU.is_ge, 0.0,
                     base=0, channel_multiplier=-1)
    PL.affine_select(identf[:], identf[:], [[-1, 128]], ALU.is_ge, 0.0,
                     base=0, channel_multiplier=1)
    PL.memset(trimask[:], 1.0)
    PL.affine_select(trimask[:], trimask[:], [[1, 128]], ALU.is_ge, 0.0,
                     base=0, channel_multiplier=-1)
    PL.memset(con_eps[:], EPS)
    PL.memset(con_scl[:], SCALE)
    PL.memset(con_half[:], -0.5)
    PL.memset(ones_col[:], 1.0)
    PL.memset(ones_row[:], 1.0)
    PL.iota(iota_p[:], [[0, 1]], base=0, channel_multiplier=1,
            allow_small_or_imprecise_dtypes=True)
    e_plc = kb.inc(PL.iota(reviota[:], [[1, 32]], base=0, channel_multiplier=0,
                           allow_small_or_imprecise_dtypes=True), "pl")
    kb.wv(DVE, e_plc)
    DVE.tensor_scalar(reviota[:], reviota[:], -1.0, 32.0, ALU.mult, ALU.add)
    for (h, d), idx in bias_keys.items():
        sl = SLOPES[h]
        e_bias = kb.inc(DVE.tensor_scalar(biascols[:, idx:idx + 1], iota_p[:], sl, sl * d,
                          ALU.mult, ALU.add), "dve")
    e_sim = kb.inc(SP.dma_start(sim_sb[:], sim_h[:]), "dSim", 16)
    e_gw = kb.inc(PL.dma_start(gwT_sb[:], gwT_h[:]), "dGw", 16)
    kb.e_plc = e_plc
    kb.e_bias = e_bias

    ln_i = [0]
    ln_sq_free = [None]

    def emit_ln(xb_ap, xn_ap, e_src, e_xn_war):
        i = ln_i[0] % 2
        ln_i[0] += 1
        st = stat[i]

        def F(instr):
            e = kb.inc(instr, "dve")
            kb.wait(DVE, e[0], e[1])
            return e

        kb.wv(DVE, e_src)
        DVE.reduce_sum(st[:, 0:1], xb_ap, axis=AX.X)
        DVE.tensor_mul(sq_scr[:], xb_ap, xb_ap)
        F(DVE.reduce_sum(st[:, 1:2], sq_scr[:], axis=AX.X))
        DVE.tensor_scalar(st[:, 2:3], st[:, 0:1], 1.0 / D, None, ALU.mult)
        F(DVE.tensor_scalar(st[:, 3:4], st[:, 1:2], 1.0 / D, None, ALU.mult))
        F(DVE.tensor_tensor(st[:, 4:5], st[:, 2:3], st[:, 2:3], ALU.mult))
        e_var = kb.inc(DVE.tensor_sub(st[:, 5:6], st[:, 3:4], st[:, 4:5]),
                       "dve")
        kb.wv(ACT, e_var)
        kb.wv(ACT, kb.e_plc)
        e_ln = kb.inc(ACT.activation(st[:, 6:7], st[:, 5:6], AF.Ln,
                                     bias=con_eps[:, 0:1]), "act")
        kb.wv(ACT, e_ln)
        e_std = kb.inc(ACT.activation(st[:, 7:8], st[:, 6:7], AF.Exp,
                                      scale=con_half[:, 0:1]), "act")
        kb.wv(DVE, e_std)
        F(DVE.tensor_scalar(st[:, 4:5], st[:, 2:3], -1.0, None, ALU.mult))
        kb.wv(DVE, e_xn_war)
        return kb.inc(DVE.tensor_scalar(xn_ap, xb_ap, st[:, 4:5], st[:, 7:8],
                                        ALU.add, ALU.mult), "dve")

    tp_free = [None, None]
    tp_i = [0]

    def emit_transposes(src_fn, dst_fn, nkc, e_src):
        last = None
        for kc in range(nkc):
            i = tp_i[0] % 2
            tp_i[0] += 1
            kb.wv(PE, kb.e_plc)
            kb.wv(PE, e_src)
            kb.wv(PE, tp_free[i])
            e_t = kb.inc(PE.transpose(pt[i][:], src_fn(kc), ident_bf[:]), "pe")
            kb.wv(ACT, e_t)
            last = kb.inc(ACT.activation(dst_fn(kc), pt[i][:], AF.Copy), "act")
            tp_free[i] = last
        return last

    akctx = ExitStack()
    x_blk = [sb(f"x_blk{i}", [128, 2048], F32, akctx) for i in range(2)]
    a0ctx = ExitStack()
    xn_tm = [sb(f"xn_tm{i}", [128, 2048], BF16, a0ctx) for i in range(2)]
    qd = [sb(f"qd{m}", [128, 2048], BF16, a0ctx) for m in range(12)]
    c_fm = [sb(f"cfm{m}", [128, 2048], BF16, a0ctx) for m in range(4)]
    wbuf = [sb(f"wbuf{i}", [128, 2048], BF16, a0ctx) for i in range(2)]
    a1ctx = ExitStack()
    xn_fm = [sb(f"xnfm{k}", [128, 2048], BF16, a1ctx) for k in range(NKC)]

    blk_free = [None, None]
    xn_tm_free = [None, None]
    tb_done = [None] * NTB
    for tb in range(NTB):
        i = tb % 2
        kb.wv(SP, blk_free[i])
        e_x = kb.inc(SP.dma_start(x_blk[i][:], x_h[tb * 128:(tb + 1) * 128, :]),
                     f"dX{i}", 16)
        e_n = emit_ln(x_blk[i][:], xn_tm[i][:], e_x, xn_tm_free[i])
        blk_free[i] = e_n
        tb_done[tb] = emit_transposes(
            lambda kc, i=i: xn_tm[i][:, kc * 128:(kc + 1) * 128],
            lambda kc, tb=tb: xn_fm[kc][:, tb * 128:(tb + 1) * 128],
            NKC, e_n)
        xn_tm_free[i] = kb.cnt("pe")
        if DBG:
            kb.wv(PL, e_n)
            e_dxn = kb.inc(PL.dma_start(dbg_xn[tb * 128:(tb + 1) * 128, :],
                                        xn_tm[i][:]), "dDbg", 16)
            xn_tm_free[i] = e_dxn

    a1_load = []
    a1_group_done = []

    def a1_emit_load(g):
        if g >= 16:
            return
        if g >= 2:
            kb.wv(PL, a1_group_done[g - 2])
        src = wqdT[g] if g < 12 else wkvdT[g - 12]
        a1_load.append(kb.inc(
            PL.dma_start(wbuf[g % 2][:], src.rearrange("p a b -> p (a b)")),
            f"dA1_{g % 2}", 16))

    a1_emit_load(0)
    a1_emit_load(1)
    for g in range(16):
        dst = qd[g] if g < 12 else c_fm[g - 12]
        for ts in range(4):
            slot = g * 4 + ts
            bank = slot % 4
            kb.wv(PE, a1_load[g])
            kb.wv(PE, tb_done[ts * 4 + 3])
            kb.wv(PE, bank_free[bank])
            for kc in range(NKC):
                mm = PE.matmul(p[bank][:],
                               wbuf[g % 2][:, kc * 128:(kc + 1) * 128],
                               xn_fm[kc][:, ts * 512:(ts + 1) * 512],
                               start=(kc == 0), stop=(kc == NKC - 1))
            e_g = kb.inc(mm, "pe")
            kb.wv(ACT, e_g)
            bank_free[bank] = kb.inc(ACT.activation(dst[:, ts * 512:(ts + 1) * 512], p[bank][:], AF.Copy), "act")
        a1_group_done.append(kb.cnt("pe"))
        a1_emit_load(g + 2)
    e_a1_done = kb.cnt("pe")
    a1ctx.close()

    a2ctx = ExitStack()
    kupT_sb = [sb(f"kupT{k}", [128, 2048], BF16, a2ctx) for k in range(4)]
    vupT_sb = [sb(f"vupT{k}", [128, 2048], BF16, a2ctx) for k in range(4)]
    vh4 = [sb(f"vh4_{j}", [128, 2048], BF16, a2ctx) for j in range(4)]
    qh = [sb(f"qh{i}", [128, 2048], BF16, a2ctx) for i in range(2)]
    kh = [sb(f"kh{i}", [128, 2048], BF16, a2ctx) for i in range(2)]
    wqu_buf = [sb(f"wqu{i}", [128, 1536], BF16, a2ctx) for i in range(2)]
    PT = [sb(f"PT{k}", [128, 512], BF16, a2ctx) for k in range(16)]
    rs_f_t = sb("rs_f", [65, 512], F32, a2ctx)
    rs_hif_t = sb("rs_hif", [65, 512], F32, a2ctx)
    rs_hi_t = sb("rs_hi", [65, 512], BF16, a2ctx)
    rs_lo_t = sb("rs_lo", [65, 512], BF16, a2ctx)
    rsb_sb = [sb(f"rsb{i}", [128, 512], F32, a2ctx) for i in range(2)]
    ot_st = [sb(f"otst{i}", [128, 512], BF16, a2ctx) for i in range(2)]

    e_kup = []
    for k in range(4):
        kb.wv(PL, e_a1_done)
        kb.inc(PL.dma_start(kupT_sb[k][:], kupT_h[k]), "dKup", 16)
        e_kup.append(kb.inc(PL.dma_start(vupT_sb[k][:], vupT_h[k]), "dKup", 16))
    head_done = [None] * (H + 2)
    vh4_free = [None] * 4
    qgen_done = [None] * (H + 2)
    wqu_load = [None] * H

    def a2_emit_loads(h):
        if h >= H:
            return
        kb.wv(PL, qgen_done[h - 2] if h >= 2 else None)
        e = kb.inc(PL.dma_start(wqu_buf[h % 2][:],
                                wquT[h].rearrange("p a b -> p (a b)")),
                   f"dQv{h % 2}", 16)
        wqu_load[h] = e

    a2_emit_loads(0)
    a2_emit_loads(1)
    PT_free = [None] * 16
    ot_st_free = [None, None]
    ot_stores = []
    rs_free = [None, None]
    rsb_free = [None, None]

    for h in range(H):
        S = gsize(h)
        ng = L // S
        bi = h % 2
        for ts in range(4):
            bank = 2 + ts % 2
            kb.wv(PE, wqu_load[h])
            kb.wv(PE, bank_free[bank])
            kb.wv(PE, head_done[h - 2] if h >= 2 else None)
            for kc in range(12):
                mm = PE.matmul(p[bank][:],
                               wqu_buf[bi][:, kc * 128:(kc + 1) * 128],
                               qd[kc][:, ts * 512:(ts + 1) * 512],
                               start=(kc == 0), stop=(kc == 11))
            e_g = kb.inc(mm, "pe")
            kb.wv(ACT, e_g)
            bank_free[bank] = kb.inc(ACT.activation(qh[bi][:, ts * 512:(ts + 1) * 512], p[bank][:],
                           AF.Copy, scale=SCALE), "act")
        for ts in range(4):
            bank = 2 + ts % 2
            kb.wv(PE, e_kup[3])
            kb.wv(PE, bank_free[bank])
            for kc in range(4):
                mm = PE.matmul(p[bank][:],
                               kupT_sb[kc][:, h * 128:(h + 1) * 128],
                               c_fm[kc][:, ts * 512:(ts + 1) * 512],
                               start=(kc == 0), stop=(kc == 3))
            e_g = kb.inc(mm, "pe")
            kb.wv(ACT, e_g)
            bank_free[bank] = kb.inc(ACT.activation(kh[bi][:, ts * 512:(ts + 1) * 512], p[bank][:], AF.Copy), "act")
        if h % 4 == 0:
            for tc in range(16):
                bank = 2 + tc % 2
                kb.wv(PE, e_kup[3])
                kb.wv(PE, bank_free[bank])
                for kc in range(4):
                    mm = PE.matmul(p[bank][:],
                                   c_fm[kc][:, tc * 128:(tc + 1) * 128],
                                   vupT_sb[kc][:, (h // 4) * 512:
                                                (h // 4 + 1) * 512],
                                   start=(kc == 0), stop=(kc == 3))
                e_g = kb.inc(mm, "pe")
                kb.wv(ACT, e_g)
                for j4 in range(4):
                    mmm = ACT.activation(
                        vh4[j4][:, tc * 128:(tc + 1) * 128],
                        p[bank][:, j4 * 128:(j4 + 1) * 128], AF.Copy)
                bank_free[bank] = kb.inc(mmm, "act")
        qgen_done[h] = kb.cnt("pe")
        e_qk_ready = kb.cnt("act")
        a2_emit_loads(h + 2)

        for g in range(ng):
            Kg = (g + 1) * S // 128
            c_g = g * S + (S - 1) / 2.0
            pt_ready = [None] * Kg
            pt_msz = [None] * Kg
            pend = None
            gp = g % 2
            ob, sbk = (4, 5) if gp == 0 else (0, 1)

            def emit_sot(kbk, first, last_flag, S=S, bi=bi,
                         pt_ready=pt_ready, pt_msz=pt_msz, ob=ob, sbk=sbk):
                kb.wv(PE, kb.e_plc)
                kb.wv(PE, pt_ready[kbk])
                kb.wv(PE, pt_msz[kbk])
                if first:
                    kb.wv(PE, bank_free[sbk])
                    kb.wv(PE, bank_free[ob])
                PE.matmul(p[sbk][0:1, 0:S], ones_col[:],
                          PT[kbk][:, 0:S], start=first, stop=last_flag)
                mm = PE.matmul(p[ob][:, 0:S],
                               vh4[h % 4][:, kbk * 128:(kbk + 1) * 128],
                               PT[kbk][:, 0:S], start=first, stop=last_flag)
                PT_free[kbk] = kb.inc(mm, "pe")

            for kbk in range(Kg):
                bank = 2 + (kbk % 2)
                kb.wv(PE, e_qk_ready)
                kb.wv(PE, bank_free[bank])
                mm = PE.matmul(p[bank][:, 0:S],
                               kh[bi][:, kbk * 128:(kbk + 1) * 128],
                               qh[bi][:, g * S:(g + 1) * S],
                               start=True, stop=True)
                e_s = kb.inc(mm, "pe")
                diag = kbk * 128 >= g * S
                off = kbk * 128 - g * S if diag else 0
                if diag and off > 0:
                    kb.wv(PL, PT_free[kbk])
                    e_msz = kb.inc(PL.memset(PT[kbk][:, 0:off], 0.0), "pl")
                else:
                    e_msz = None
                kb.wv(ACT, kb.e_bias)
                kb.wv(ACT, e_s)
                kb.wv(ACT, PT_free[kbk])
                idx = bias_keys[(h, kbk * 128 - c_g)]
                e_e = kb.inc(ACT.activation(PT[kbk][:, off:S], p[bank][:, off:S], AF.Exp,
                               bias=biascols[:, idx:idx + 1]), "act")
                bank_free[bank] = e_e
                if diag:
                    kb.wv(DVE, kb.e_plc)
                    kb.wv(DVE, e_e)
                    e_pt = kb.inc(DVE.tensor_tensor(PT[kbk][:, off:off + 128],
                                      PT[kbk][:, off:off + 128],
                                      trimask[:], ALU.mult), "dve")
                else:
                    e_pt = e_e
                pt_ready[kbk] = e_pt
                pt_msz[kbk] = e_msz
                if pend is not None:
                    emit_sot(pend, pend == 0, pend == Kg - 1)
                pend = kbk
            emit_sot(pend, pend == 0, pend == Kg - 1)
            e_grp = kb.cnt("pe")
            kb.wv(DVE, e_grp)
            kb.wv(DVE, rs_free[gp])
            rp = 64 * gp
            rsf = rs_f_t[rp:rp + 1, 0:S]
            rshi = rs_hi_t[rp:rp + 1, 0:S]
            rshif = rs_hif_t[rp:rp + 1, 0:S]
            rslo = rs_lo_t[rp:rp + 1, 0:S]
            kb.fence(DVE, DVE.reciprocal(rsf, p[sbk][0:1, 0:S]), "dve")
            kb.fence(DVE, DVE.tensor_copy(rshi, rsf), "dve")
            kb.fence(DVE, DVE.tensor_copy(rshif, rshi), "dve")
            kb.fence(DVE, DVE.tensor_sub(rsf, rsf, rshif), "dve")
            e_rs = kb.inc(DVE.tensor_copy(rslo, rsf), "dve")
            if DBG:
                kb.wv(PL, kb.cnt("dve"))
                kb.inc(PL.dma_start(dbg_s[h, g * S:(g + 1) * S][None, :],
                                    rsf), "dDbg", 16)
            kb.wv(PE, e_rs)
            PE.matmul(p[sbk][:, 0:S], ones_row[rp:rp + 1, :], rshi,
                      start=True, stop=False)
            mm = PE.matmul(p[sbk][:, 0:S], ones_row[rp:rp + 1, :], rslo,
                           start=False, stop=True)
            e_rsb = kb.inc(mm, "pe")
            rs_free[gp] = e_rsb
            kb.wv(ACT, e_rsb)
            kb.wv(ACT, rsb_free[gp])
            e_rc = kb.inc(ACT.activation(rsb_sb[gp][:, 0:S], p[sbk][:, 0:S],
                                         AF.Copy), "act")
            bank_free[sbk] = e_rc
            kb.wv(DVE, e_rc)
            kb.wv(DVE, ot_st_free[gp])
            e_oev = kb.inc(DVE.tensor_mul(ot_st[gp][:, 0:S], p[ob][:, 0:S],
                           rsb_sb[gp][:, 0:S]), "dve")
            rsb_free[gp] = e_oev
            bank_free[ob] = e_oev
            kb.wv(SP, e_oev)
            e_st = kb.inc(
                SP.dma_start(ots[h * 128:(h + 1) * 128, g * S:(g + 1) * S],
                             ot_st[g % 2][:, 0:S]), f"dOts{g % 2}", 16)
            ot_st_free[g % 2] = e_st
            ot_stores.append(e_st)
        head_done[h] = kb.cnt("pe")
        vh4_free[h % 4] = head_done[h]
    e_ots_all = [kb.cnt("dOts0"), kb.cnt("dOts1")]
    a2ctx.close()
    a0ctx.close()

    xns = nc.dram_tensor("xns", [8, 128, 2048], BF16, kind="Internal")
    bkctx = ExitStack()
    xn1 = [sb(f"xn1_{k}", [128, 1024], BF16, bkctx) for k in range(NKC)]
    cw = [sb(f"cw{t}", [128, 32], F32, bkctx) for t in range(16)]
    btctx = ExitStack()
    bn_tm = [sb(f"bn_tm{i}", [128, 2048], BF16, btctx) for i in range(2)]
    xn1b = [sb(f"xn1b{i}", [128, 2048], BF16, btctx) for i in range(2)]
    rt = []
    for i in range(2):
        rt.append({
            "probs": sb(f"probs{i}", [128, 32], F32, btctx),
            "pwork": sb(f"pwork{i}", [128, 32], F32, btctx),
            "wvals": sb(f"wvals{i}", [128, 8], F32, btctx),
            "ohs": [sb(f"ohs{i}_{j}", [128, 32], F32, btctx) for j in range(8)],
            "gcol": sb(f"gcol{i}", [128, 16], F32, btctx),
            "rr1": [sb(f"rr1_{i}_{j}", [128, 32], F32, btctx) for j in range(4)],
            "bp": [sb(f"bp{i}_{j}", [128, 32], F32, btctx) for j in range(4)],
            "scol": sb(f"scol{i}", [128, 64], F32, btctx),
        })
    sjT = [sb(f"sjT{j}", [32, 128], F32, btctx) for j in range(4)]
    a3ctx = ExitStack()
    woT_sb = [sb(f"woT{k}", [128, 2048], BF16, a3ctx) for k in range(16)]
    otb = [sb(f"otb{i}", [128, 8192], BF16, a3ctx) for i in range(2)]
    x1_st = [sb(f"x1st{i}", [128, 2048], F32, a3ctx) for i in range(2)]
    kb.wv(PL, e_ots_all[0])
    kb.wv(PL, e_ots_all[1])
    e_wo = [None] * 4
    for kc in range(16):
        e_wo[kc // 4] = kb.inc(PL.dma_start(woT_sb[kc][:], woT_h[kc]),
                               f"dWo{kc // 4}", 16)
    otb_load = [None] * 4
    otb_free = [None, None]
    bn_free = [None, None]
    x1_st_free = [[], []]
    x1_written = [None] * NTB
    xn1b_store = [None, None]

    def a3_load_otb(g4):
        if g4 >= 4:
            return
        kb.wv(SP, e_ots_all[0])
        kb.wv(SP, e_ots_all[1])
        kb.wv(SP, otb_free[g4 % 2])
        for kc in range(16):
            e = kb.inc(SP.dma_start(
                otb[g4 % 2][:, kc * 512:(kc + 1) * 512],
                ots[kc * 128:(kc + 1) * 128, g4 * 512:(g4 + 1) * 512]),
                f"dOtb{g4 % 2}", 16)
        otb_load[g4] = e

    def emit_route_extract(tc, e_tp):
        par = tc % 2
        R = rt[par]
        scol, probs, pwork, wvals, ohs = (
            R["scol"], R["probs"], R["pwork"], R["wvals"], R["ohs"])
        rr1 = R["rr1"][0]
        if tc < 8:
            def lh(kc, tc=tc):
                return xn1[kc][:, tc * 128:(tc + 1) * 128]
        else:
            def lh(kc, par=par):
                return xn1b[par][:, kc * 128:(kc + 1) * 128]
        kb.wv(PE, e_tp)
        kb.wv(PE, e_gw)
        kb.wv(PE, bank_free[4])
        for kc in range(NKC):
            mm = PE.matmul(p[4][:, 0:32], lh(kc),
                           gwT_sb[:, kc * 32:(kc + 1) * 32],
                           start=(kc == 0), stop=(kc == NKC - 1))
        e_lg = kb.inc(mm, "pe")
        kb.wv(DVE, e_lg)
        kb.wv(DVE, e_bias)

        def F(instr):
            e = kb.inc(instr, "dve")
            kb.wait(DVE, e[0], e[1])
            return e

        F(DVE.reduce_max(scol[:, 0:1], p[4][:, 0:32], axis=AX.X, negate=True))
        e_mx = kb.cnt("dve")
        kb.wv(ACT, e_mx)
        e_pr = kb.inc(ACT.activation(probs[:], p[4][:, 0:32], AF.Exp,
                                     bias=scol[:, 0:1]), "act")
        bank_free[4] = e_pr
        kb.wv(DVE, e_pr)
        F(DVE.reduce_sum(scol[:, 1:2], probs[:], axis=AX.X))
        F(DVE.reciprocal(scol[:, 2:3], scol[:, 1:2]))
        F(DVE.tensor_single_scalar(probs[:], probs[:], scol[:, 2:3], ALU.mult))
        F(DVE.tensor_copy(pwork[:], probs[:]))
        for j in range(8):
            F(DVE.reduce_max(scol[:, 3:4], pwork[:], axis=AX.X))
            F(DVE.tensor_single_scalar(rr1[:], pwork[:], scol[:, 3:4],
                                       ALU.is_equal))
            F(DVE.tensor_mul(rr1[:], rr1[:], reviota[:]))
            F(DVE.reduce_max(scol[:, 4:5], rr1[:], axis=AX.X))
            F(DVE.tensor_single_scalar(ohs[j][:], rr1[:], scol[:, 4:5],
                                       ALU.is_equal))
            DVE.tensor_copy(wvals[:, j:j + 1], scol[:, 3:4])
            F(DVE.scalar_tensor_tensor(pwork[:], ohs[j][:], -2.0, pwork[:],
                                       ALU.mult, ALU.add))
        if DBG:
            kb.wv(PL, kb.cnt("dve"))
            kb.inc(PL.dma_start(dbg_pr[tc * 128:(tc + 1) * 128, :],
                                probs[:]), "dDbg", 16)
            kb.inc(PL.dma_start(dbg_wv[tc * 128:(tc + 1) * 128, :],
                                wvals[:]), "dDbg", 16)
        F(DVE.reduce_sum(scol[:, 5:6], wvals[:], axis=AX.X))
        F(DVE.reciprocal(scol[:, 6:7], scol[:, 5:6]))

    def emit_route_gather(tc):
        par = tc % 2
        R = rt[par]
        wvals, ohs, gcol = R["wvals"], R["ohs"], R["gcol"]
        e_top = kb.cnt("dve")
        kb.wv(PE, e_top)
        kb.wv(PE, bank_free[4])
        for j in range(4):
            e_t = kb.inc(PE.transpose(p[4][0:32, j * 128:(j + 1) * 128],
                                      ohs[4 + j][:], identf[:]), "pe")
        kb.wv(DVE, e_t)
        for j in range(4):
            e_sjT = kb.inc(DVE.tensor_copy(sjT[j][:],
                                           p[4][0:32, j * 128:(j + 1) * 128]),
                           "dve")
        bank_free[4] = e_sjT
        kb.wv(PE, e_sjT)
        kb.wv(PE, e_sim)
        kb.wv(PE, bank_free[5])
        for j in range(4):
            e_gmm = kb.inc(PE.matmul(p[5][:, j * 32:(j + 1) * 32], sjT[j][:],
                                     sim_sb[:], start=(j == 0), stop=(j == 3)),
                           "pe")
        kb.wv(DVE, e_gmm)
        chains = []
        for j in range(4):
            sc = R["scol"][:, j * 16:(j + 1) * 16]
            gc = gcol[:, j * 4:(j + 1) * 4]
            rr = R["rr1"][j]
            bp = R["bp"][j]
            oj = ohs[4 + j]
            ops = []
            for i4 in range(4):
                ops.append((DVE.tensor_mul,
                            (rr[:], p[5][:, j * 32:(j + 1) * 32], ohs[i4][:])))
                ops.append((DVE.reduce_sum, (gc[:, i4:i4 + 1], rr[:]),
                            {"axis": AX.X}))
            ops.append((DVE.reduce_max, (sc[:, 7:8], gc[:]), {"axis": AX.X}))
            ops.append((DVE.tensor_tensor,
                        (sc[:, 9:10], gc[:, 0:1], sc[:, 7:8], ALU.is_equal)))
            ops.append((DVE.tensor_scalar,
                        (sc[:, 10:11], sc[:, 9:10], -1.0, 1.0, ALU.mult,
                         ALU.add)))
            ops.append((DVE.tensor_tensor,
                        (sc[:, 11:12], gc[:, 1:2], sc[:, 7:8], ALU.is_equal)))
            ops.append((DVE.tensor_mul,
                        (sc[:, 11:12], sc[:, 11:12], sc[:, 10:11])))
            ops.append((DVE.tensor_sub,
                        (sc[:, 10:11], sc[:, 10:11], sc[:, 11:12])))
            ops.append((DVE.tensor_tensor,
                        (sc[:, 12:13], gc[:, 2:3], sc[:, 7:8], ALU.is_equal)))
            ops.append((DVE.tensor_mul,
                        (sc[:, 12:13], sc[:, 12:13], sc[:, 10:11])))
            ops.append((DVE.tensor_sub,
                        (sc[:, 10:11], sc[:, 10:11], sc[:, 12:13])))
            ops.append((DVE.tensor_tensor,
                        (sc[:, 13:14], gc[:, 3:4], sc[:, 7:8], ALU.is_equal)))
            ops.append((DVE.tensor_mul,
                        (sc[:, 13:14], sc[:, 13:14], sc[:, 10:11])))
            ops.append((DVE.tensor_single_scalar,
                        (bp[:], ohs[0][:], sc[:, 9:10], ALU.mult)))
            ops.append((DVE.scalar_tensor_tensor,
                        (bp[:], ohs[1][:], sc[:, 11:12], bp[:], ALU.mult,
                         ALU.add)))
            ops.append((DVE.scalar_tensor_tensor,
                        (bp[:], ohs[2][:], sc[:, 12:13], bp[:], ALU.mult,
                         ALU.add)))
            ops.append((DVE.scalar_tensor_tensor,
                        (bp[:], ohs[3][:], sc[:, 13:14], bp[:], ALU.mult,
                         ALU.add)))
            ops.append((DVE.tensor_single_scalar,
                        (sc[:, 14:15], sc[:, 7:8], 0.5, ALU.is_ge)))
            ops.append((DVE.tensor_scalar,
                        (sc[:, 15:16], sc[:, 14:15], -1.0, 1.0, ALU.mult,
                         ALU.add)))
            ops.append((DVE.tensor_single_scalar,
                        (oj[:], oj[:], sc[:, 15:16], ALU.mult)))
            ops.append((DVE.scalar_tensor_tensor,
                        (oj[:], bp[:], sc[:, 14:15], oj[:], ALU.mult,
                         ALU.add)))
            chains.append(ops)
        for step in range(max(len(c) for c in chains)):
            for j in range(4):
                if step < len(chains[j]):
                    op = chains[j][step]
                    fn, args = op[0], op[1]
                    kw = op[2] if len(op) > 2 else {}
                    e_ch = kb.inc(fn(*args, **kw), "dve")
        bank_free[5] = e_ch

        def F(instr):
            e = kb.inc(instr, "dve")
            kb.wait(DVE, e[0], e[1])
            return e

        F(DVE.tensor_single_scalar(cw[tc][:], ohs[0][:], wvals[:, 0:1],
                                   ALU.mult))
        for j in range(1, 8):
            F(DVE.scalar_tensor_tensor(cw[tc][:], ohs[j][:], wvals[:, j:j + 1],
                                       cw[tc][:], ALU.mult, ALU.add))
        kb.inc(DVE.tensor_single_scalar(cw[tc][:], cw[tc][:],
                                        R["scol"][:, 6:7], ALU.mult), "dve")
        if DBG:
            kb.wv(SP, kb.cnt("dve"))
            kb.inc(SP.dma_start(dbg_cw[tc * 128:(tc + 1) * 128, :],
                                cw[tc][:]), "dDbg", 16)

    a3_load_otb(0)
    a3_load_otb(1)
    xld_evt = [None, None]
    route_pend = None
    for qb in range(NTB):
        g4, qoff = qb // 4, qb % 4
        kb.wv(SP, blk_free[qb % 2])
        xld_evt[qb % 2] = kb.inc(
            SP.dma_start(x_blk[qb % 2][:], x_h[qb * 128:(qb + 1) * 128, :]),
            f"dX{qb % 2}", 16)
        e_add = None
        for n in range(4):
            bank = (qb * 4 + n) % 4
            kb.wv(PE, otb_load[g4])
            kb.wv(PE, bank_free[bank])
            for kc in range(16):
                kb.wv(PE, e_wo[kc // 4])
                mm = PE.matmul(
                    p[bank][:],
                    otb[g4 % 2][:, kc * 512 + qoff * 128: kc * 512 + (qoff + 1) * 128],
                    woT_sb[kc][:, n * 512:(n + 1) * 512],
                    start=(kc == 0), stop=(kc == 15))
            e_z = kb.inc(mm, "pe")
            kb.wv(DVE, e_z)
            kb.wv(DVE, xld_evt[qb % 2])
            for ev in x1_st_free[qb % 2]:
                kb.wv(DVE, ev)
            e_add = kb.inc(DVE.tensor_add(x1_st[qb % 2][:, n * 512:(n + 1) * 512],
                           p[bank][:], x_blk[qb % 2][:, n * 512:(n + 1) * 512]), "dve")
            bank_free[bank] = e_add
        blk_free[qb % 2] = e_add
        kb.wv(SP, e_add)
        e_w = kb.inc(SP.dma_start(x1s[qb * 128:(qb + 1) * 128, :],
                                  x1_st[qb % 2][:]), f"dX1s{qb % 2}", 16)
        x1_written[qb] = e_w
        if qoff == 3:
            otb_free[g4 % 2] = kb.cnt("pe")
            a3_load_otb(g4 + 2)
        e_n = emit_ln(x1_st[qb % 2][:], bn_tm[qb % 2][:], None,
                      bn_free[qb % 2])
        x1_st_free[qb % 2] = [e_w, e_n]
        if qb < 8:
            def dstf(kc, qb=qb):
                return xn1[kc][:, qb * 128:(qb + 1) * 128]
        else:
            kb.wv(ACT, xn1b_store[qb % 2])

            def dstf(kc, qb=qb):
                return xn1b[qb % 2][:, kc * 128:(kc + 1) * 128]
        e_tp = emit_transposes(
            lambda kc, i=qb % 2: bn_tm[i][:, kc * 128:(kc + 1) * 128],
            dstf, NKC, e_n)
        bn_free[qb % 2] = kb.cnt("pe")
        if qb >= 8:
            kb.wv(SP, e_tp)
            xn1b_store[qb % 2] = kb.inc(
                SP.dma_start(xns[qb - 8], xn1b[qb % 2][:]), f"dXb{qb % 2}", 16)
        emit_route_extract(qb, e_tp)
        if route_pend is not None:
            emit_route_gather(route_pend)
        route_pend = qb
    emit_route_gather(route_pend)
    if DBG:
        kb.wv(PL, kb.cnt("dve"))
        for kc in range(NKC):
            kb.inc(PL.dma_start(dbg_xn1[0, kc * 128:(kc + 1) * 128, :],
                                xn1[kc][:]), "dDbg", 16)
    e_route_pe = kb.cnt("pe")
    e_route_dve = kb.cnt("dve")
    a3ctx.close()
    btctx.close()

    bctx = ExitStack()
    y_T = [sb(f"yT{t}", [128, 2048], F32, bctx) for t in range(8)]
    h_sb = [sb(f"hsb{j}", [128, 1024], BF16, bctx) for j in range(6)]
    wgbuf = [sb(f"wgbuf{i}", [128, 2048], BF16, bctx) for i in range(2)]
    wdbuf = [sb(f"wdbuf{i}", [128, 12288], BF16, bctx) for i in range(2)]
    sg = [sb(f"sg{i}", [128, 512], BF16, bctx) for i in range(2)]

    y_store = [[None] * 8, [None] * 8]
    y_last = [None] * 8

    total_g = 2 * NE * 12
    gu_load = [None] * total_g
    gu_done = [None] * total_g

    def b2_gu_load(gg):
        if gg >= total_g:
            return
        hf2, rem = divmod(gg, NE * 12)
        e, idx = divmod(rem, 12)
        m = M_SEQ[idx]
        if gg >= 2:
            kb.wv(PL, gu_done[gg - 2])
        else:
            kb.wv(PL, x1_written[15])
            kb.wv(PL, x1_written[14])
            kb.wv(PL, xn1b_store[0])
            kb.wv(PL, xn1b_store[1])
            kb.wv(PL, e_route_pe)
            kb.wv(PL, e_route_dve)
        gu_load[gg] = kb.inc(
            PL.dma_start(wgbuf[gg % 2][:], wg_h[e, m].rearrange("p a b -> p (a b)")),
            f"dWg{gg % 2}", 16)

    wd_load = [None] * (2 * NE)
    wd_done = [None] * (2 * NE)

    def b2_wd_load(ee):
        if ee >= 2 * NE:
            return
        e = ee % NE
        if ee >= 2:
            kb.wv(PL, wd_done[ee - 2])
        else:
            kb.wv(PL, x1_written[15])
            kb.wv(PL, x1_written[14])
            kb.wv(PL, xn1b_store[0])
            kb.wv(PL, xn1b_store[1])
            kb.wv(PL, e_route_pe)
            kb.wv(PL, e_route_dve)
        wd_load[ee] = kb.inc(
            PL.dma_start(
                wdbuf[ee % 2][:].rearrange("p (k n) -> p k n", k=6),
                wd_h[e].rearrange("k p n -> p k n")),
            f"dWd{ee % 2}", 16)

    b2_gu_load(0)
    b2_gu_load(1)
    b2_wd_load(0)
    b2_wd_load(1)

    sg_free = [None, None]
    h_free = [None] * 6
    h_evt = [None] * 6

    for hf in range(2):
        kb.wv(SP, x1_written[14])
        kb.wv(SP, x1_written[15])
        kb.wv(SP, xn1b_store[0])
        kb.wv(SP, xn1b_store[1])
        kb.wv(SP, kb.cnt("dve"))
        kb.wv(SP, kb.cnt("pe"))
        if hf == 1:
            kb.wv(SP, y_store[0][7])
        for tb in range(8):
            kb.inc(SP.dma_start(
                y_T[tb][:], x1s[hf * 1024 + tb * 128: hf * 1024 + (tb + 1) * 128, :]),
                "dYl", 16)
        e_yl = kb.cnt("dYl")
        if hf == 1:
            kb.wv(PL, gu_done[NE * 12 - 1])
            kb.wv(PL, xn1b_store[0])
            kb.wv(PL, xn1b_store[1])
            for kc in range(NKC):
                e_xnr = kb.inc(PL.dma_start(
                    xn1[kc][:].rearrange("p (t f) -> p t f", t=8),
                    xns[:, :, kc * 128:(kc + 1) * 128].rearrange(
                        "t p f -> p t f")), "dKup", 16)
        for e in range(NE):
            ee = hf * NE + e
            for pj in range(6):
                pair_evts = []
                for part in range(2):
                    idx = 2 * pj + part
                    gg = ee * 12 + idx
                    m = M_SEQ[idx]
                    for ts in range(2):
                        bank = 2 * part + ts
                        kb.wv(PE, gu_load[gg])
                        if hf == 1:
                            kb.wv(PE, e_xnr)
                        kb.wv(PE, bank_free[bank])
                        for kc in range(NKC):
                            mm = PE.matmul(
                                p[bank][:],
                                wgbuf[gg % 2][:, kc * 128:(kc + 1) * 128],
                                xn1[kc][:, ts * 512:(ts + 1) * 512],
                                start=(kc == 0), stop=(kc == NKC - 1))
                        pair_evts.append(kb.inc(mm, "pe"))
                    gu_done[gg] = kb.cnt("pe")
                    b2_gu_load(gg + 2)
                for ts in range(2):
                    kb.wv(ACT, pair_evts[ts])
                    kb.wv(ACT, sg_free[ts])
                    e_sg = kb.inc(ACT.activation(sg[ts][:], p[ts][:], AF.Silu), "act")
                    bank_free[ts] = e_sg
                    kb.wv(DVE, e_sg)
                    kb.wv(DVE, pair_evts[2 + ts])
                    kb.wv(DVE, h_free[pj])
                    e_h = kb.inc(DVE.tensor_mul(h_sb[pj][:, ts * 512:(ts + 1) * 512],
                                   sg[ts][:], p[2 + ts][:]), "dve")
                    bank_free[2 + ts] = e_h
                    sg_free[ts] = e_h
                h_evt[pj] = kb.cnt("dve")
            e_h_all = kb.cnt("dve")
            if DBG and hf == 0 and e == 0:
                kb.wv(PL, e_h_all)
                for pj6 in range(6):
                    e_dh = kb.inc(PL.dma_start(
                        dbg_h[pj6 * 128:(pj6 + 1) * 128, :], h_sb[pj6][:]),
                        "dDbg", 16)
                for pj6 in range(6):
                    h_free[pj6] = e_dh
            for tc in range(8):
                for n in range(4):
                    k2 = tc * 4 + n
                    bank = 4 + (k2 % 2)
                    kb.wv(PE, wd_load[ee])
                    kb.wv(PE, bank_free[bank])
                    for kd in range(6):
                        kb.wv(PE, h_evt[kd])
                        mm = PE.matmul(
                            p[bank][:],
                            h_sb[kd][:, tc * 128:(tc + 1) * 128],
                            wdbuf[ee % 2][:, kd * 2048 + n * 512:
                                          kd * 2048 + (n + 1) * 512],
                            start=(kd == 0), stop=(kd == 5))
                    e_z = kb.inc(mm, "pe")
                    kb.wv(DVE, e_z)
                    kb.wv(DVE, e_yl)
                    e_y = kb.inc(DVE.scalar_tensor_tensor(
                        y_T[tc][:, n * 512:(n + 1) * 512], p[bank][:],
                        cw[hf * 8 + tc][:, e:e + 1],
                        y_T[tc][:, n * 512:(n + 1) * 512],
                        ALU.mult, ALU.add), "dve")
                    if DBG and hf == 0 and e == 0 and tc == 0 and n == 3:
                        kb.wv(PL, e_y)
                        kb.inc(PL.dma_start(dbg_y0[:, :], y_T[0][:]),
                               "dDbg", 16)
                    bank_free[bank] = e_y
                    if e == NE - 1:
                        y_last[tc] = e_y
            wd_done[ee] = kb.cnt("pe")
            b2_wd_load(ee + 2)
            for pj in range(6):
                h_free[pj] = wd_done[ee]
        for tb in range(8):
            kb.wv(SP, y_last[tb])
            y_store[hf][tb] = kb.inc(
                SP.dma_start(out_h[hf * 1024 + tb * 128:
                                   hf * 1024 + (tb + 1) * 128, :], y_T[tb][:]),
                "dOut", 16)

    if DBG:
        kb.inc(SP.dma_start(dbg_x1[:, :], x1s[:, :]), "dDbg", 16)
        e_db = kb.inc(PL.dma_start(
            dbg_ot.rearrange("(a p) q -> a p q", p=128),
            ots.rearrange("(a p) q -> a p q", p=128)), "dDbg", 16)
        kb.wait(SP, "dDbg", kb.cnt("dDbg")[1])
    kb.wait(SP, "dOut", kb.cnt("dOut")[1])
    kb.wait(PL, "dOut", kb.cnt("dOut")[1])
    bctx.close()
    bkctx.close()
    akctx.close()
    pctx.close()
    top.close()
    nc.finalize()
    return nc


def kb_act_evt(e):
    return e


def kb_ds_evt(e):
    return e


def kb_dve_evt(e):
    return e


def _prep(inputs):
    f = np.float32
    x = np.ascontiguousarray(np.asarray(inputs["L0_attn_hidden_states"], f))
    wqd = np.asarray(inputs["L0_attn_Wq_down"], f)
    wqu = np.asarray(inputs["L0_attn_Wq_up"], f)
    wkvd = np.asarray(inputs["L0_attn_Wkv_down"], f)
    kup = np.asarray(inputs["L0_attn_k_up"], f)
    vup = np.asarray(inputs["L0_attn_v_up"], f)
    wo = np.asarray(inputs["L0_attn_Wo"], f)
    gw = np.asarray(inputs["L0_ffn_gate_weight"], f)
    gup = np.asarray(inputs["L0_ffn_gate_up_proj"], f)
    dwn = np.asarray(inputs["L0_ffn_down_proj"], f)
    sim = np.ascontiguousarray(np.asarray(inputs["L0_ffn_similarity"], f))

    shared = {
        "wqdT": np.ascontiguousarray(
            wqd.T.reshape(16, 128, 12, 128).transpose(2, 1, 0, 3)),
        "wquT": np.ascontiguousarray(
            wqu.T.reshape(12, 128, 16, 128).transpose(2, 1, 0, 3)),
        "wkvdT": np.ascontiguousarray(
            wkvd.T.reshape(16, 128, 4, 128).transpose(2, 1, 0, 3)),
        "kupT": np.ascontiguousarray(
            kup.transpose(1, 0, 2).reshape(512, 2048).reshape(4, 128, 2048)),
        "vupT": np.ascontiguousarray(
            vup.transpose(1, 0, 2).reshape(512, 2048).reshape(4, 128, 2048)),
        "woT": np.ascontiguousarray(wo.T).reshape(16, 128, 2048),
        "gwT": np.ascontiguousarray(
            gw.T.reshape(16, 128, 32).transpose(1, 0, 2).reshape(128, 512)),
        "sim": sim,
        "wg": np.ascontiguousarray(
            gup.transpose(0, 2, 1).reshape(32, 16, 128, 12, 128)
            .transpose(0, 3, 2, 1, 4)),
        "wd": np.ascontiguousarray(dwn.transpose(0, 2, 1)).reshape(32, 6, 128, 2048),
    }
    return x, shared


def kernel(**inputs):
    x, shared = _prep(inputs)
    nc = build()
    in_maps = []
    for core in range(8):
        m = {"x": np.ascontiguousarray(x[core])}
        m.update(shared)
        in_maps.append(m)
    res = run_bass_kernel_spmd(nc, in_maps, core_ids=list(range(8)))
    return np.stack([r["out"] for r in res.results], axis=0)


# version 09
# speedup vs baseline: 1.1549x; 1.1549x over previous
import math
import numpy as np
import concourse.bass as bass
import concourse.mybir as mybir
from concourse.bass_utils import run_bass_kernel_spmd
from contextlib import ExitStack

F32 = mybir.dt.float32
BF16 = mybir.dt.bfloat16
AF = mybir.ActivationFunctionType
ALU = mybir.AluOpType
AX = mybir.AxisListType

B, L, D = 8, 2048, 2048
H, HD = 16, 128
RQ, RKV = 1536, 512
NE, DFF = 32, 768
SCALE = 1.0 / math.sqrt(HD)
_BASE = 2.0 ** (-2.0 ** (-(math.log2(H) - 3)))
SLOPES = [_BASE ** (i + 1) for i in range(H)]
NKC = D // 128
NTB = L // 128
EPS = 1e-5
M_SEQ = [0, 6, 1, 7, 2, 8, 3, 9, 4, 10, 5, 11]
DBG = False


def gsize(h):
    return 128 if h == 0 else (256 if h <= 2 else 512)


class KB:
    def __init__(self):
        self.nc = bass.Bass("TRN2", target_bir_lowering=False)
        self.sems = {}
        self.hwm = {}

    def sem(self, name):
        self.sems[name] = [self.nc.alloc_semaphore(name), 0]

    def inc(self, instr, name, n=1):
        rec = self.sems[name]
        instr.then_inc(rec[0], n)
        rec[1] += n
        return (name, rec[1])

    def cnt(self, name):
        return (name, self.sems[name][1])

    def wait(self, eng, name, val):
        if val is None or val <= 0:
            return
        rec = self.sems[name]
        assert val <= rec[1], (name, val, rec[1])
        key = (id(eng), name)
        if self.hwm.get(key, 0) >= val:
            return
        eng.wait_ge(rec[0], val)
        self.hwm[key] = val

    def wv(self, eng, evt):
        if evt is not None:
            self.wait(eng, evt[0], evt[1])

    def fence(self, eng, instr, sem):
        e = self.inc(instr, sem)
        self.wait(eng, e[0], e[1])
        return e


def build():
    kb = KB()
    nc = kb.nc
    PE, ACT, DVE, PL, SP = nc.tensor, nc.scalar, nc.vector, nc.gpsimd, nc.sync

    x_h = nc.dram_tensor("x", [L, D], F32, kind="ExternalInput")
    wqdT = nc.dram_tensor("wqdT", [12, 128, 16, 128], F32, kind="ExternalInput")
    wquT = nc.dram_tensor("wquT", [16, 128, 12, 128], F32, kind="ExternalInput")
    wkvdT = nc.dram_tensor("wkvdT", [4, 128, 16, 128], F32, kind="ExternalInput")
    kupT_h = nc.dram_tensor("kupT", [4, 128, 2048], F32, kind="ExternalInput")
    vupT_h = nc.dram_tensor("vupT", [4, 128, 2048], F32, kind="ExternalInput")
    woT_h = nc.dram_tensor("woT", [16, 128, 2048], F32, kind="ExternalInput")
    gwT_h = nc.dram_tensor("gwT", [128, 512], F32, kind="ExternalInput")
    sim_h = nc.dram_tensor("sim", [32, 32], F32, kind="ExternalInput")
    wg_h = nc.dram_tensor("wg", [32, 12, 128, 16, 128], F32, kind="ExternalInput")
    wd_h = nc.dram_tensor("wd", [32, 6, 128, 2048], F32, kind="ExternalInput")
    x1s = nc.dram_tensor("x1s", [L, D], F32, kind="Internal")
    ots = nc.dram_tensor("ots", [D, L], BF16, kind="Internal")
    out_h = nc.dram_tensor("out", [L, D], F32, kind="ExternalOutput")
    if DBG:
        dbg_x1 = nc.dram_tensor("dbg_x1", [L, D], F32, kind="ExternalOutput")
        dbg_ot = nc.dram_tensor("dbg_ot", [D, L], F32, kind="ExternalOutput")
        dbg_cw = nc.dram_tensor("dbg_cw", [L, 32], F32, kind="ExternalOutput")
        dbg_xn = nc.dram_tensor("dbg_xn", [L, D], F32, kind="ExternalOutput")
        dbg_pr = nc.dram_tensor("dbg_pr", [L, 32], F32, kind="ExternalOutput")
        dbg_s = nc.dram_tensor("dbg_s", [16, L], F32, kind="ExternalOutput")
        dbg_xn1 = nc.dram_tensor("dbg_xn1", [2, D, 1024], F32, kind="ExternalOutput")
        dbg_h = nc.dram_tensor("dbg_h", [768, 1024], F32, kind="ExternalOutput")
        dbg_y0 = nc.dram_tensor("dbg_y0", [128, 2048], F32, kind="ExternalOutput")
        dbg_wv = nc.dram_tensor("dbg_wv", [L, 8], F32, kind="ExternalOutput")

    for s in ["pe", "act", "dve", "pl", "dSim", "dGw", "dX0", "dX1",
              "dA1_0", "dA1_1", "dKup", "dQv0", "dQv1", "dOts0", "dOts1",
              "dWo0", "dWo1", "dWo2", "dWo3",
              "dOtb0", "dOtb1", "dX1s0", "dX1s1", "dXb0", "dXb1", "dYl",
              "dWg0", "dWg1", "dWd0", "dWd1", "dOut", "dDbg"]:
        kb.sem(s)
    for rec in kb.sems.values():
        nc.gpsimd.sem_clear(rec[0])
    nc.all_engine_barrier()

    top = ExitStack()

    def sb(name, shape, dtype, ctx):
        return ctx.enter_context(nc.sbuf_tensor(name, shape, dtype))

    ident_bf = sb("ident_bf", [128, 128], BF16, top)
    identf = sb("identf", [128, 128], F32, top)
    trimask = sb("trimask", [128, 128], BF16, top)
    ones_col = sb("ones_col", [128, 1], BF16, top)
    ones_row = sb("ones_row", [65, 128], BF16, top)
    iota_p = sb("iota_p", [128, 1], F32, top)
    reviota = sb("reviota", [128, 32], F32, top)
    sim_sb = sb("sim_sb", [32, 32], F32, top)
    gwT_sb = sb("gwT_sb", [128, 512], BF16, top)
    sq_scr = sb("sq_scr", [128, 2048], F32, top)
    con_eps = sb("con_eps", [128, 1], F32, top)
    con_scl = sb("con_scl", [128, 1], F32, top)
    con_half = sb("con_half", [128, 1], F32, top)
    stat = [sb(f"stat{i}", [128, 8], F32, top) for i in range(2)]

    bias_keys = {}
    for h in range(H):
        S = gsize(h)
        for g in range(L // S):
            c = g * S + (S - 1) / 2.0
            for kbk in range((g + 1) * S // 128):
                bias_keys.setdefault((h, kbk * 128 - c), len(bias_keys))
    NBC = len(bias_keys)
    biascols = sb("biascols", [128, NBC], F32, top)

    pctx = ExitStack()
    p = [pctx.enter_context(nc.psum_tensor(f"p{i}", [128, 512], F32))
         for i in range(6)]
    pt = [pctx.enter_context(nc.psum_tensor(f"pt{i}", [128, 128], BF16))
          for i in range(2)]
    bank_free = [None] * 6

    PL.memset(ident_bf[:], 1.0)
    PL.affine_select(ident_bf[:], ident_bf[:], [[1, 128]], ALU.is_ge, 0.0,
                     base=0, channel_multiplier=-1)
    PL.affine_select(ident_bf[:], ident_bf[:], [[-1, 128]], ALU.is_ge, 0.0,
                     base=0, channel_multiplier=1)
    PL.memset(identf[:], 1.0)
    PL.affine_select(identf[:], identf[:], [[1, 128]], ALU.is_ge, 0.0,
                     base=0, channel_multiplier=-1)
    PL.affine_select(identf[:], identf[:], [[-1, 128]], ALU.is_ge, 0.0,
                     base=0, channel_multiplier=1)
    PL.memset(trimask[:], 1.0)
    PL.affine_select(trimask[:], trimask[:], [[1, 128]], ALU.is_ge, 0.0,
                     base=0, channel_multiplier=-1)
    PL.memset(con_eps[:], EPS)
    PL.memset(con_scl[:], SCALE)
    PL.memset(con_half[:], -0.5)
    PL.memset(ones_col[:], 1.0)
    PL.memset(ones_row[:], 1.0)
    PL.iota(iota_p[:], [[0, 1]], base=0, channel_multiplier=1,
            allow_small_or_imprecise_dtypes=True)
    e_plc = kb.inc(PL.iota(reviota[:], [[1, 32]], base=0, channel_multiplier=0,
                           allow_small_or_imprecise_dtypes=True), "pl")
    kb.wv(DVE, e_plc)
    DVE.tensor_scalar(reviota[:], reviota[:], -1.0, 32.0, ALU.mult, ALU.add)
    for (h, d), idx in bias_keys.items():
        sl = SLOPES[h]
        e_bias = kb.inc(DVE.tensor_scalar(biascols[:, idx:idx + 1], iota_p[:], sl, sl * d,
                          ALU.mult, ALU.add), "dve")
    e_sim = kb.inc(SP.dma_start(sim_sb[:], sim_h[:]), "dSim", 16)
    e_gw = kb.inc(PL.dma_start(gwT_sb[:], gwT_h[:]), "dGw", 16)
    kb.e_plc = e_plc
    kb.e_bias = e_bias

    ln_i = [0]
    ln_sq_free = [None]

    def emit_ln(xb_ap, xn_ap, e_src, e_xn_war):
        i = ln_i[0] % 2
        ln_i[0] += 1
        st = stat[i]

        def F(instr):
            e = kb.inc(instr, "dve")
            kb.wait(DVE, e[0], e[1])
            return e

        kb.wv(ACT, e_src)
        kb.wv(ACT, ln_sq_free[0])
        e_sq = kb.inc(ACT.activation(sq_scr[:], xb_ap, AF.Square), "act")
        kb.wv(DVE, e_src)
        DVE.reduce_sum(st[:, 0:1], xb_ap, axis=AX.X)
        kb.wv(DVE, e_sq)
        ln_sq_free[0] = kb.fence(
            DVE, DVE.reduce_sum(st[:, 1:2], sq_scr[:], axis=AX.X), "dve")
        DVE.tensor_scalar(st[:, 2:3], st[:, 0:1], 1.0 / D, None, ALU.mult)
        F(DVE.tensor_scalar(st[:, 3:4], st[:, 1:2], 1.0 / D, None, ALU.mult))
        F(DVE.tensor_tensor(st[:, 4:5], st[:, 2:3], st[:, 2:3], ALU.mult))
        e_var = kb.inc(DVE.tensor_sub(st[:, 5:6], st[:, 3:4], st[:, 4:5]),
                       "dve")
        kb.wv(ACT, e_var)
        kb.wv(ACT, kb.e_plc)
        e_ln = kb.inc(ACT.activation(st[:, 6:7], st[:, 5:6], AF.Ln,
                                     bias=con_eps[:, 0:1]), "act")
        kb.wv(ACT, e_ln)
        e_std = kb.inc(ACT.activation(st[:, 7:8], st[:, 6:7], AF.Exp,
                                      scale=con_half[:, 0:1]), "act")
        kb.wv(DVE, e_std)
        F(DVE.tensor_scalar(st[:, 4:5], st[:, 2:3], -1.0, None, ALU.mult))
        kb.wv(DVE, e_xn_war)
        return kb.inc(DVE.tensor_scalar(xn_ap, xb_ap, st[:, 4:5], st[:, 7:8],
                                        ALU.add, ALU.mult), "dve")

    tp_free = [None, None]
    tp_i = [0]

    def emit_transposes(src_fn, dst_fn, nkc, e_src):
        last = None
        for kc in range(nkc):
            i = tp_i[0] % 2
            tp_i[0] += 1
            kb.wv(PE, kb.e_plc)
            kb.wv(PE, e_src)
            kb.wv(PE, tp_free[i])
            e_t = kb.inc(PE.transpose(pt[i][:], src_fn(kc), ident_bf[:]), "pe")
            kb.wv(ACT, e_t)
            last = kb.inc(ACT.activation(dst_fn(kc), pt[i][:], AF.Copy), "act")
            tp_free[i] = last
        return last

    akctx = ExitStack()
    x_blk = [sb(f"x_blk{i}", [128, 2048], F32, akctx) for i in range(2)]
    a0ctx = ExitStack()
    xn_tm = [sb(f"xn_tm{i}", [128, 2048], BF16, a0ctx) for i in range(2)]
    qd = [sb(f"qd{m}", [128, 2048], BF16, a0ctx) for m in range(12)]
    c_fm = [sb(f"cfm{m}", [128, 2048], BF16, a0ctx) for m in range(4)]
    wbuf = [sb(f"wbuf{i}", [128, 2048], BF16, a0ctx) for i in range(2)]
    a1ctx = ExitStack()
    xn_fm = [sb(f"xnfm{k}", [128, 2048], BF16, a1ctx) for k in range(NKC)]

    blk_free = [None, None]
    xn_tm_free = [None, None]
    tb_done = [None] * NTB
    a1_load = [None] * 64
    a1_group_done = [None] * 64

    def a1_emit_load(gg):
        if gg >= 64:
            return
        if gg >= 2:
            kb.wv(PL, a1_group_done[gg - 2])
        g16 = gg % 16
        srcw = wqdT[g16] if g16 < 12 else wkvdT[g16 - 12]
        a1_load[gg] = kb.inc(
            PL.dma_start(wbuf[gg % 2][:], srcw.rearrange("p a b -> p (a b)")),
            f"dA1_{gg % 2}", 16)

    a1_emit_load(0)
    a1_emit_load(1)
    for tb in range(NTB):
        i = tb % 2
        kb.wv(SP, blk_free[i])
        e_x = kb.inc(SP.dma_start(x_blk[i][:], x_h[tb * 128:(tb + 1) * 128, :]),
                     f"dX{i}", 16)
        e_n = emit_ln(x_blk[i][:], xn_tm[i][:], e_x, xn_tm_free[i])
        blk_free[i] = e_n
        tb_done[tb] = emit_transposes(
            lambda kc, i=i: xn_tm[i][:, kc * 128:(kc + 1) * 128],
            lambda kc, tb=tb: xn_fm[kc][:, tb * 128:(tb + 1) * 128],
            NKC, e_n)
        xn_tm_free[i] = kb.cnt("pe")
        if DBG:
            kb.wv(PL, e_n)
            e_dxn = kb.inc(PL.dma_start(dbg_xn[tb * 128:(tb + 1) * 128, :],
                                        xn_tm[i][:]), "dDbg", 16)
            xn_tm_free[i] = e_dxn
        if tb % 4 == 3:
            ts = tb // 4
            for g16 in range(16):
                gg = ts * 16 + g16
                dst = qd[g16] if g16 < 12 else c_fm[g16 - 12]
                bank = gg % 4
                kb.wv(PE, a1_load[gg])
                kb.wv(PE, tb_done[tb])
                kb.wv(PE, bank_free[bank])
                for kc in range(NKC):
                    mm = PE.matmul(p[bank][:],
                                   wbuf[gg % 2][:, kc * 128:(kc + 1) * 128],
                                   xn_fm[kc][:, ts * 512:(ts + 1) * 512],
                                   start=(kc == 0), stop=(kc == NKC - 1))
                e_g = kb.inc(mm, "pe")
                kb.wv(ACT, e_g)
                bank_free[bank] = kb.inc(
                    ACT.activation(dst[:, ts * 512:(ts + 1) * 512], p[bank][:],
                                   AF.Copy), "act")
                a1_group_done[gg] = kb.cnt("pe")
                a1_emit_load(gg + 2)
    e_a1_done = kb.cnt("pe")
    a1ctx.close()

    a2ctx = ExitStack()
    kupT_sb = [sb(f"kupT{k}", [128, 2048], BF16, a2ctx) for k in range(4)]
    vupT_sb = [sb(f"vupT{k}", [128, 2048], BF16, a2ctx) for k in range(4)]
    vh4 = [sb(f"vh4_{j}", [128, 2048], BF16, a2ctx) for j in range(4)]
    qh = [sb(f"qh{i}", [128, 2048], BF16, a2ctx) for i in range(2)]
    kh = [sb(f"kh{i}", [128, 2048], BF16, a2ctx) for i in range(2)]
    wqu_buf = [sb(f"wqu{i}", [128, 1536], BF16, a2ctx) for i in range(2)]
    PT = [sb(f"PT{k}", [128, 512], BF16, a2ctx) for k in range(16)]
    rs_f_t = sb("rs_f", [65, 512], F32, a2ctx)
    rs_hif_t = sb("rs_hif", [65, 512], F32, a2ctx)
    rs_hi_t = sb("rs_hi", [65, 512], BF16, a2ctx)
    rs_lo_t = sb("rs_lo", [65, 512], BF16, a2ctx)
    rsb_sb = [sb(f"rsb{i}", [128, 512], F32, a2ctx) for i in range(2)]
    ot_st = [sb(f"otst{i}", [128, 512], BF16, a2ctx) for i in range(2)]

    e_kup = []
    for k in range(4):
        kb.wv(PL, e_a1_done)
        kb.inc(PL.dma_start(kupT_sb[k][:], kupT_h[k]), "dKup", 16)
        e_kup.append(kb.inc(PL.dma_start(vupT_sb[k][:], vupT_h[k]), "dKup", 16))
    head_done = [None] * (H + 2)
    vh4_free = [None] * 4
    qgen_done = [None] * (H + 2)
    wqu_load = [None] * H

    def a2_emit_loads(h):
        if h >= H:
            return
        kb.wv(PL, qgen_done[h - 2] if h >= 2 else None)
        e = kb.inc(PL.dma_start(wqu_buf[h % 2][:],
                                wquT[h].rearrange("p a b -> p (a b)")),
                   f"dQv{h % 2}", 16)
        wqu_load[h] = e

    a2_emit_loads(0)
    a2_emit_loads(1)
    PT_free = [None] * 16
    ot_st_free = [None, None]
    ot_stores = []
    rs_free = [None, None]
    rsb_free = [None, None]

    for h in range(H):
        S = gsize(h)
        ng = L // S
        bi = h % 2
        for ts in range(4):
            bank = 2 + ts % 2
            kb.wv(PE, wqu_load[h])
            kb.wv(PE, bank_free[bank])
            kb.wv(PE, head_done[h - 2] if h >= 2 else None)
            for kc in range(12):
                mm = PE.matmul(p[bank][:],
                               wqu_buf[bi][:, kc * 128:(kc + 1) * 128],
                               qd[kc][:, ts * 512:(ts + 1) * 512],
                               start=(kc == 0), stop=(kc == 11))
            e_g = kb.inc(mm, "pe")
            kb.wv(ACT, e_g)
            bank_free[bank] = kb.inc(ACT.activation(qh[bi][:, ts * 512:(ts + 1) * 512], p[bank][:],
                           AF.Copy, scale=SCALE), "act")
        for ts in range(4):
            bank = 2 + ts % 2
            kb.wv(PE, e_kup[3])
            kb.wv(PE, bank_free[bank])
            for kc in range(4):
                mm = PE.matmul(p[bank][:],
                               kupT_sb[kc][:, h * 128:(h + 1) * 128],
                               c_fm[kc][:, ts * 512:(ts + 1) * 512],
                               start=(kc == 0), stop=(kc == 3))
            e_g = kb.inc(mm, "pe")
            kb.wv(ACT, e_g)
            bank_free[bank] = kb.inc(ACT.activation(kh[bi][:, ts * 512:(ts + 1) * 512], p[bank][:], AF.Copy), "act")
        if h % 4 == 0:
            for tc in range(16):
                bank = 2 + tc % 2
                kb.wv(PE, e_kup[3])
                kb.wv(PE, bank_free[bank])
                for kc in range(4):
                    mm = PE.matmul(p[bank][:],
                                   c_fm[kc][:, tc * 128:(tc + 1) * 128],
                                   vupT_sb[kc][:, (h // 4) * 512:
                                                (h // 4 + 1) * 512],
                                   start=(kc == 0), stop=(kc == 3))
                e_g = kb.inc(mm, "pe")
                kb.wv(ACT, e_g)
                for j4 in range(4):
                    mmm = ACT.activation(
                        vh4[j4][:, tc * 128:(tc + 1) * 128],
                        p[bank][:, j4 * 128:(j4 + 1) * 128], AF.Copy)
                bank_free[bank] = kb.inc(mmm, "act")
        qgen_done[h] = kb.cnt("pe")
        e_qk_ready = kb.cnt("act")
        a2_emit_loads(h + 2)

        for g in range(ng):
            Kg = (g + 1) * S // 128
            c_g = g * S + (S - 1) / 2.0
            pt_ready = [None] * Kg
            pt_msz = [None] * Kg
            pend = None
            gp = g % 2
            ob, sbk = (4, 5) if gp == 0 else (0, 1)

            def emit_sot(kbk, first, last_flag, S=S, bi=bi,
                         pt_ready=pt_ready, pt_msz=pt_msz, ob=ob, sbk=sbk):
                kb.wv(PE, kb.e_plc)
                kb.wv(PE, pt_ready[kbk])
                kb.wv(PE, pt_msz[kbk])
                if first:
                    kb.wv(PE, bank_free[sbk])
                    kb.wv(PE, bank_free[ob])
                PE.matmul(p[sbk][0:1, 0:S], ones_col[:],
                          PT[kbk][:, 0:S], start=first, stop=last_flag)
                mm = PE.matmul(p[ob][:, 0:S],
                               vh4[h % 4][:, kbk * 128:(kbk + 1) * 128],
                               PT[kbk][:, 0:S], start=first, stop=last_flag)
                PT_free[kbk] = kb.inc(mm, "pe")

            for kbk in range(Kg):
                bank = 2 + (kbk % 2)
                kb.wv(PE, e_qk_ready)
                kb.wv(PE, bank_free[bank])
                mm = PE.matmul(p[bank][:, 0:S],
                               kh[bi][:, kbk * 128:(kbk + 1) * 128],
                               qh[bi][:, g * S:(g + 1) * S],
                               start=True, stop=True)
                e_s = kb.inc(mm, "pe")
                diag = kbk * 128 >= g * S
                off = kbk * 128 - g * S if diag else 0
                if diag and off > 0:
                    kb.wv(PL, PT_free[kbk])
                    e_msz = kb.inc(PL.memset(PT[kbk][:, 0:off], 0.0), "pl")
                else:
                    e_msz = None
                kb.wv(ACT, kb.e_bias)
                kb.wv(ACT, e_s)
                kb.wv(ACT, PT_free[kbk])
                idx = bias_keys[(h, kbk * 128 - c_g)]
                e_e = kb.inc(ACT.activation(PT[kbk][:, off:S], p[bank][:, off:S], AF.Exp,
                               bias=biascols[:, idx:idx + 1]), "act")
                bank_free[bank] = e_e
                if diag:
                    kb.wv(DVE, kb.e_plc)
                    kb.wv(DVE, e_e)
                    e_pt = kb.inc(DVE.tensor_tensor(PT[kbk][:, off:off + 128],
                                      PT[kbk][:, off:off + 128],
                                      trimask[:], ALU.mult), "dve")
                else:
                    e_pt = e_e
                pt_ready[kbk] = e_pt
                pt_msz[kbk] = e_msz
                if pend is not None:
                    emit_sot(pend, pend == 0, pend == Kg - 1)
                pend = kbk
            emit_sot(pend, pend == 0, pend == Kg - 1)
            e_grp = kb.cnt("pe")
            kb.wv(DVE, e_grp)
            kb.wv(DVE, rs_free[gp])
            rp = 64 * gp
            rsf = rs_f_t[rp:rp + 1, 0:S]
            rshi = rs_hi_t[rp:rp + 1, 0:S]
            rshif = rs_hif_t[rp:rp + 1, 0:S]
            rslo = rs_lo_t[rp:rp + 1, 0:S]
            kb.fence(DVE, DVE.reciprocal(rsf, p[sbk][0:1, 0:S]), "dve")
            kb.fence(DVE, DVE.tensor_copy(rshi, rsf), "dve")
            kb.fence(DVE, DVE.tensor_copy(rshif, rshi), "dve")
            kb.fence(DVE, DVE.tensor_sub(rsf, rsf, rshif), "dve")
            e_rs = kb.inc(DVE.tensor_copy(rslo, rsf), "dve")
            if DBG:
                kb.wv(PL, kb.cnt("dve"))
                kb.inc(PL.dma_start(dbg_s[h, g * S:(g + 1) * S][None, :],
                                    rsf), "dDbg", 16)
            kb.wv(PE, e_rs)
            PE.matmul(p[sbk][:, 0:S], ones_row[rp:rp + 1, :], rshi,
                      start=True, stop=False)
            mm = PE.matmul(p[sbk][:, 0:S], ones_row[rp:rp + 1, :], rslo,
                           start=False, stop=True)
            e_rsb = kb.inc(mm, "pe")
            rs_free[gp] = e_rsb
            kb.wv(ACT, e_rsb)
            kb.wv(ACT, rsb_free[gp])
            e_rc = kb.inc(ACT.activation(rsb_sb[gp][:, 0:S], p[sbk][:, 0:S],
                                         AF.Copy), "act")
            bank_free[sbk] = e_rc
            kb.wv(DVE, e_rc)
            kb.wv(DVE, ot_st_free[gp])
            e_oev = kb.inc(DVE.tensor_mul(ot_st[gp][:, 0:S], p[ob][:, 0:S],
                           rsb_sb[gp][:, 0:S]), "dve")
            rsb_free[gp] = e_oev
            bank_free[ob] = e_oev
            kb.wv(SP, e_oev)
            e_st = kb.inc(
                SP.dma_start(ots[h * 128:(h + 1) * 128, g * S:(g + 1) * S],
                             ot_st[g % 2][:, 0:S]), f"dOts{g % 2}", 16)
            ot_st_free[g % 2] = e_st
            ot_stores.append(e_st)
        head_done[h] = kb.cnt("pe")
        vh4_free[h % 4] = head_done[h]
    e_ots_all = [kb.cnt("dOts0"), kb.cnt("dOts1")]
    a2ctx.close()
    a0ctx.close()

    xns = nc.dram_tensor("xns", [8, 128, 2048], BF16, kind="Internal")
    bkctx = ExitStack()
    xn1 = [sb(f"xn1_{k}", [128, 1024], BF16, bkctx) for k in range(NKC)]
    cw = [sb(f"cw{t}", [128, 32], F32, bkctx) for t in range(16)]
    btctx = ExitStack()
    bn_tm = [sb(f"bn_tm{i}", [128, 2048], BF16, btctx) for i in range(2)]
    xn1b = [sb(f"xn1b{i}", [128, 2048], BF16, btctx) for i in range(2)]
    rt = []
    for i in range(2):
        rt.append({
            "probs": sb(f"probs{i}", [128, 32], F32, btctx),
            "pwork": sb(f"pwork{i}", [128, 32], F32, btctx),
            "wvals": sb(f"wvals{i}", [128, 8], F32, btctx),
            "ohs": [sb(f"ohs{i}_{j}", [128, 32], F32, btctx) for j in range(8)],
            "gcol": sb(f"gcol{i}", [128, 16], F32, btctx),
            "rr1": [sb(f"rr1_{i}_{j}", [128, 32], F32, btctx) for j in range(4)],
            "bp": [sb(f"bp{i}_{j}", [128, 32], F32, btctx) for j in range(4)],
            "scol": sb(f"scol{i}", [128, 64], F32, btctx),
        })
    sjT = [sb(f"sjT{j}", [32, 128], F32, btctx) for j in range(4)]
    a3ctx = ExitStack()
    woT_sb = [sb(f"woT{k}", [128, 2048], BF16, a3ctx) for k in range(16)]
    otb = [sb(f"otb{i}", [128, 8192], BF16, a3ctx) for i in range(2)]
    x1_st = [sb(f"x1st{i}", [128, 2048], F32, a3ctx) for i in range(2)]
    kb.wv(PL, e_ots_all[0])
    kb.wv(PL, e_ots_all[1])
    e_wo = [None] * 4
    for kc in range(16):
        e_wo[kc // 4] = kb.inc(PL.dma_start(woT_sb[kc][:], woT_h[kc]),
                               f"dWo{kc // 4}", 16)
    otb_load = [None] * 4
    otb_free = [None, None]
    bn_free = [None, None]
    x1_st_free = [[], []]
    x1_written = [None] * NTB
    xn1b_store = [None, None]

    def a3_load_otb(g4):
        if g4 >= 4:
            return
        kb.wv(SP, e_ots_all[0])
        kb.wv(SP, e_ots_all[1])
        kb.wv(SP, otb_free[g4 % 2])
        for kc in range(16):
            e = kb.inc(SP.dma_start(
                otb[g4 % 2][:, kc * 512:(kc + 1) * 512],
                ots[kc * 128:(kc + 1) * 128, g4 * 512:(g4 + 1) * 512]),
                f"dOtb{g4 % 2}", 16)
        otb_load[g4] = e

    def emit_route_extract(tc, e_tp):
        par = tc % 2
        R = rt[par]
        scol, probs, pwork, wvals, ohs = (
            R["scol"], R["probs"], R["pwork"], R["wvals"], R["ohs"])
        rr1 = R["rr1"][0]
        if tc < 8:
            def lh(kc, tc=tc):
                return xn1[kc][:, tc * 128:(tc + 1) * 128]
        else:
            def lh(kc, par=par):
                return xn1b[par][:, kc * 128:(kc + 1) * 128]
        kb.wv(PE, e_tp)
        kb.wv(PE, e_gw)
        kb.wv(PE, bank_free[4])
        for kc in range(NKC):
            mm = PE.matmul(p[4][:, 0:32], lh(kc),
                           gwT_sb[:, kc * 32:(kc + 1) * 32],
                           start=(kc == 0), stop=(kc == NKC - 1))
        e_lg = kb.inc(mm, "pe")
        kb.wv(DVE, e_lg)
        kb.wv(DVE, e_bias)

        def F(instr):
            e = kb.inc(instr, "dve")
            kb.wait(DVE, e[0], e[1])
            return e

        F(DVE.reduce_max(scol[:, 0:1], p[4][:, 0:32], axis=AX.X, negate=True))
        e_mx = kb.cnt("dve")
        kb.wv(ACT, e_mx)
        e_pr = kb.inc(ACT.activation(probs[:], p[4][:, 0:32], AF.Exp,
                                     bias=scol[:, 0:1]), "act")
        bank_free[4] = e_pr
        kb.wv(DVE, e_pr)
        F(DVE.reduce_sum(scol[:, 1:2], probs[:], axis=AX.X))
        F(DVE.reciprocal(scol[:, 2:3], scol[:, 1:2]))
        F(DVE.tensor_single_scalar(probs[:], probs[:], scol[:, 2:3], ALU.mult))
        F(DVE.tensor_copy(pwork[:], probs[:]))
        for j in range(8):
            F(DVE.reduce_max(scol[:, 3:4], pwork[:], axis=AX.X))
            F(DVE.tensor_single_scalar(rr1[:], pwork[:], scol[:, 3:4],
                                       ALU.is_equal))
            F(DVE.tensor_mul(rr1[:], rr1[:], reviota[:]))
            F(DVE.reduce_max(scol[:, 4:5], rr1[:], axis=AX.X))
            F(DVE.tensor_single_scalar(ohs[j][:], rr1[:], scol[:, 4:5],
                                       ALU.is_equal))
            DVE.tensor_copy(wvals[:, j:j + 1], scol[:, 3:4])
            F(DVE.scalar_tensor_tensor(pwork[:], ohs[j][:], -2.0, pwork[:],
                                       ALU.mult, ALU.add))
        if DBG:
            kb.wv(PL, kb.cnt("dve"))
            kb.inc(PL.dma_start(dbg_pr[tc * 128:(tc + 1) * 128, :],
                                probs[:]), "dDbg", 16)
            kb.inc(PL.dma_start(dbg_wv[tc * 128:(tc + 1) * 128, :],
                                wvals[:]), "dDbg", 16)
        F(DVE.reduce_sum(scol[:, 5:6], wvals[:], axis=AX.X))
        F(DVE.reciprocal(scol[:, 6:7], scol[:, 5:6]))

    def emit_route_gather(tc):
        par = tc % 2
        R = rt[par]
        wvals, ohs, gcol = R["wvals"], R["ohs"], R["gcol"]
        e_top = kb.cnt("dve")
        kb.wv(PE, e_top)
        kb.wv(PE, bank_free[4])
        for j in range(4):
            e_t = kb.inc(PE.transpose(p[4][0:32, j * 128:(j + 1) * 128],
                                      ohs[4 + j][:], identf[:]), "pe")
        kb.wv(DVE, e_t)
        for j in range(4):
            e_sjT = kb.inc(DVE.tensor_copy(sjT[j][:],
                                           p[4][0:32, j * 128:(j + 1) * 128]),
                           "dve")
        bank_free[4] = e_sjT
        kb.wv(PE, e_sjT)
        kb.wv(PE, e_sim)
        kb.wv(PE, bank_free[5])
        for j in range(4):
            e_gmm = kb.inc(PE.matmul(p[5][:, j * 32:(j + 1) * 32], sjT[j][:],
                                     sim_sb[:], start=(j == 0), stop=(j == 3)),
                           "pe")
        kb.wv(DVE, e_gmm)
        chains = []
        for j in range(4):
            sc = R["scol"][:, j * 16:(j + 1) * 16]
            gc = gcol[:, j * 4:(j + 1) * 4]
            rr = R["rr1"][j]
            bp = R["bp"][j]
            oj = ohs[4 + j]
            ops = []
            for i4 in range(4):
                ops.append((DVE.tensor_mul,
                            (rr[:], p[5][:, j * 32:(j + 1) * 32], ohs[i4][:])))
                ops.append((DVE.reduce_sum, (gc[:, i4:i4 + 1], rr[:]),
                            {"axis": AX.X}))
            ops.append((DVE.reduce_max, (sc[:, 7:8], gc[:]), {"axis": AX.X}))
            ops.append((DVE.tensor_tensor,
                        (sc[:, 9:10], gc[:, 0:1], sc[:, 7:8], ALU.is_equal)))
            ops.append((DVE.tensor_scalar,
                        (sc[:, 10:11], sc[:, 9:10], -1.0, 1.0, ALU.mult,
                         ALU.add)))
            ops.append((DVE.tensor_tensor,
                        (sc[:, 11:12], gc[:, 1:2], sc[:, 7:8], ALU.is_equal)))
            ops.append((DVE.tensor_mul,
                        (sc[:, 11:12], sc[:, 11:12], sc[:, 10:11])))
            ops.append((DVE.tensor_sub,
                        (sc[:, 10:11], sc[:, 10:11], sc[:, 11:12])))
            ops.append((DVE.tensor_tensor,
                        (sc[:, 12:13], gc[:, 2:3], sc[:, 7:8], ALU.is_equal)))
            ops.append((DVE.tensor_mul,
                        (sc[:, 12:13], sc[:, 12:13], sc[:, 10:11])))
            ops.append((DVE.tensor_sub,
                        (sc[:, 10:11], sc[:, 10:11], sc[:, 12:13])))
            ops.append((DVE.tensor_tensor,
                        (sc[:, 13:14], gc[:, 3:4], sc[:, 7:8], ALU.is_equal)))
            ops.append((DVE.tensor_mul,
                        (sc[:, 13:14], sc[:, 13:14], sc[:, 10:11])))
            ops.append((DVE.tensor_single_scalar,
                        (bp[:], ohs[0][:], sc[:, 9:10], ALU.mult)))
            ops.append((DVE.scalar_tensor_tensor,
                        (bp[:], ohs[1][:], sc[:, 11:12], bp[:], ALU.mult,
                         ALU.add)))
            ops.append((DVE.scalar_tensor_tensor,
                        (bp[:], ohs[2][:], sc[:, 12:13], bp[:], ALU.mult,
                         ALU.add)))
            ops.append((DVE.scalar_tensor_tensor,
                        (bp[:], ohs[3][:], sc[:, 13:14], bp[:], ALU.mult,
                         ALU.add)))
            ops.append((DVE.tensor_single_scalar,
                        (sc[:, 14:15], sc[:, 7:8], 0.5, ALU.is_ge)))
            ops.append((DVE.tensor_scalar,
                        (sc[:, 15:16], sc[:, 14:15], -1.0, 1.0, ALU.mult,
                         ALU.add)))
            ops.append((DVE.tensor_single_scalar,
                        (oj[:], oj[:], sc[:, 15:16], ALU.mult)))
            ops.append((DVE.scalar_tensor_tensor,
                        (oj[:], bp[:], sc[:, 14:15], oj[:], ALU.mult,
                         ALU.add)))
            chains.append(ops)
        for step in range(max(len(c) for c in chains)):
            for j in range(4):
                if step < len(chains[j]):
                    op = chains[j][step]
                    fn, args = op[0], op[1]
                    kw = op[2] if len(op) > 2 else {}
                    e_ch = kb.inc(fn(*args, **kw), "dve")
        bank_free[5] = e_ch

        def F(instr):
            e = kb.inc(instr, "dve")
            kb.wait(DVE, e[0], e[1])
            return e

        F(DVE.tensor_single_scalar(cw[tc][:], ohs[0][:], wvals[:, 0:1],
                                   ALU.mult))
        for j in range(1, 8):
            F(DVE.scalar_tensor_tensor(cw[tc][:], ohs[j][:], wvals[:, j:j + 1],
                                       cw[tc][:], ALU.mult, ALU.add))
        kb.inc(DVE.tensor_single_scalar(cw[tc][:], cw[tc][:],
                                        R["scol"][:, 6:7], ALU.mult), "dve")
        if DBG:
            kb.wv(SP, kb.cnt("dve"))
            kb.inc(SP.dma_start(dbg_cw[tc * 128:(tc + 1) * 128, :],
                                cw[tc][:]), "dDbg", 16)

    a3_load_otb(0)
    a3_load_otb(1)
    xld_evt = [None, None]
    route_pend = None
    for qb in range(NTB):
        g4, qoff = qb // 4, qb % 4
        kb.wv(SP, blk_free[qb % 2])
        xld_evt[qb % 2] = kb.inc(
            SP.dma_start(x_blk[qb % 2][:], x_h[qb * 128:(qb + 1) * 128, :]),
            f"dX{qb % 2}", 16)
        e_add = None
        for n in range(4):
            bank = (qb * 4 + n) % 4
            kb.wv(PE, otb_load[g4])
            kb.wv(PE, bank_free[bank])
            for kc in range(16):
                kb.wv(PE, e_wo[kc // 4])
                mm = PE.matmul(
                    p[bank][:],
                    otb[g4 % 2][:, kc * 512 + qoff * 128: kc * 512 + (qoff + 1) * 128],
                    woT_sb[kc][:, n * 512:(n + 1) * 512],
                    start=(kc == 0), stop=(kc == 15))
            e_z = kb.inc(mm, "pe")
            kb.wv(DVE, e_z)
            kb.wv(DVE, xld_evt[qb % 2])
            for ev in x1_st_free[qb % 2]:
                kb.wv(DVE, ev)
            e_add = kb.inc(DVE.tensor_add(x1_st[qb % 2][:, n * 512:(n + 1) * 512],
                           p[bank][:], x_blk[qb % 2][:, n * 512:(n + 1) * 512]), "dve")
            bank_free[bank] = e_add
        blk_free[qb % 2] = e_add
        kb.wv(SP, e_add)
        e_w = kb.inc(SP.dma_start(x1s[qb * 128:(qb + 1) * 128, :],
                                  x1_st[qb % 2][:]), f"dX1s{qb % 2}", 16)
        x1_written[qb] = e_w
        if qoff == 3:
            otb_free[g4 % 2] = kb.cnt("pe")
            a3_load_otb(g4 + 2)
        e_n = emit_ln(x1_st[qb % 2][:], bn_tm[qb % 2][:], e_add,
                      bn_free[qb % 2])
        x1_st_free[qb % 2] = [e_w, e_n]
        if qb < 8:
            def dstf(kc, qb=qb):
                return xn1[kc][:, qb * 128:(qb + 1) * 128]
        else:
            kb.wv(ACT, xn1b_store[qb % 2])

            def dstf(kc, qb=qb):
                return xn1b[qb % 2][:, kc * 128:(kc + 1) * 128]
        e_tp = emit_transposes(
            lambda kc, i=qb % 2: bn_tm[i][:, kc * 128:(kc + 1) * 128],
            dstf, NKC, e_n)
        bn_free[qb % 2] = kb.cnt("pe")
        if qb >= 8:
            kb.wv(SP, e_tp)
            xn1b_store[qb % 2] = kb.inc(
                SP.dma_start(xns[qb - 8], xn1b[qb % 2][:]), f"dXb{qb % 2}", 16)
        emit_route_extract(qb, e_tp)
        if route_pend is not None:
            emit_route_gather(route_pend)
        route_pend = qb
    emit_route_gather(route_pend)
    if DBG:
        kb.wv(PL, kb.cnt("dve"))
        for kc in range(NKC):
            kb.inc(PL.dma_start(dbg_xn1[0, kc * 128:(kc + 1) * 128, :],
                                xn1[kc][:]), "dDbg", 16)
    e_route_pe = kb.cnt("pe")
    e_route_dve = kb.cnt("dve")
    a3ctx.close()
    btctx.close()

    bctx = ExitStack()
    y_T = [sb(f"yT{t}", [128, 2048], F32, bctx) for t in range(8)]
    h_sb = [sb(f"hsb{j}", [128, 1024], BF16, bctx) for j in range(6)]
    wgbuf = [sb(f"wgbuf{i}", [128, 2048], BF16, bctx) for i in range(2)]
    wdbuf = [sb(f"wdbuf{i}", [128, 12288], BF16, bctx) for i in range(2)]
    sg = [sb(f"sg{i}", [128, 512], BF16, bctx) for i in range(2)]

    y_store = [[None] * 8, [None] * 8]
    y_last = [None] * 8

    total_g = 2 * NE * 12
    gu_load = [None] * total_g
    gu_done = [None] * total_g

    def b2_gu_load(gg):
        if gg >= total_g:
            return
        hf2, rem = divmod(gg, NE * 12)
        e, idx = divmod(rem, 12)
        m = M_SEQ[idx]
        if gg >= 2:
            kb.wv(PL, gu_done[gg - 2])
        else:
            kb.wv(PL, x1_written[15])
            kb.wv(PL, x1_written[14])
            kb.wv(PL, xn1b_store[0])
            kb.wv(PL, xn1b_store[1])
            kb.wv(PL, e_route_pe)
            kb.wv(PL, e_route_dve)
        gu_load[gg] = kb.inc(
            PL.dma_start(wgbuf[gg % 2][:], wg_h[e, m].rearrange("p a b -> p (a b)")),
            f"dWg{gg % 2}", 16)

    wd_load = [None] * (2 * NE)
    wd_done = [None] * (2 * NE)

    def b2_wd_load(ee):
        if ee >= 2 * NE:
            return
        e = ee % NE
        if ee >= 2:
            kb.wv(PL, wd_done[ee - 2])
        else:
            kb.wv(PL, x1_written[15])
            kb.wv(PL, x1_written[14])
            kb.wv(PL, xn1b_store[0])
            kb.wv(PL, xn1b_store[1])
            kb.wv(PL, e_route_pe)
            kb.wv(PL, e_route_dve)
        wd_load[ee] = kb.inc(
            PL.dma_start(
                wdbuf[ee % 2][:].rearrange("p (k n) -> p k n", k=6),
                wd_h[e].rearrange("k p n -> p k n")),
            f"dWd{ee % 2}", 16)

    b2_gu_load(0)
    b2_gu_load(1)
    b2_wd_load(0)
    b2_wd_load(1)

    sg_free = [None, None]
    h_free = [None] * 6
    h_evt = [None] * 6

    for hf in range(2):
        kb.wv(SP, x1_written[14])
        kb.wv(SP, x1_written[15])
        kb.wv(SP, xn1b_store[0])
        kb.wv(SP, xn1b_store[1])
        kb.wv(SP, kb.cnt("dve"))
        kb.wv(SP, kb.cnt("pe"))
        if hf == 1:
            kb.wv(SP, y_store[0][7])
        for tb in range(8):
            kb.inc(SP.dma_start(
                y_T[tb][:], x1s[hf * 1024 + tb * 128: hf * 1024 + (tb + 1) * 128, :]),
                "dYl", 16)
        e_yl = kb.cnt("dYl")
        if hf == 1:
            kb.wv(PL, gu_done[NE * 12 - 1])
            kb.wv(PL, xn1b_store[0])
            kb.wv(PL, xn1b_store[1])
            for kc in range(NKC):
                e_xnr = kb.inc(PL.dma_start(
                    xn1[kc][:].rearrange("p (t f) -> p t f", t=8),
                    xns[:, :, kc * 128:(kc + 1) * 128].rearrange(
                        "t p f -> p t f")), "dKup", 16)
        for e in range(NE):
            ee = hf * NE + e
            for pj in range(6):
                pair_evts = []
                for part in range(2):
                    idx = 2 * pj + part
                    gg = ee * 12 + idx
                    m = M_SEQ[idx]
                    for ts in range(2):
                        bank = 2 * part + ts
                        kb.wv(PE, gu_load[gg])
                        if hf == 1:
                            kb.wv(PE, e_xnr)
                        kb.wv(PE, bank_free[bank])
                        for kc in range(NKC):
                            mm = PE.matmul(
                                p[bank][:],
                                wgbuf[gg % 2][:, kc * 128:(kc + 1) * 128],
                                xn1[kc][:, ts * 512:(ts + 1) * 512],
                                start=(kc == 0), stop=(kc == NKC - 1))
                        pair_evts.append(kb.inc(mm, "pe"))
                    gu_done[gg] = kb.cnt("pe")
                    b2_gu_load(gg + 2)
                for ts in range(2):
                    kb.wv(ACT, pair_evts[ts])
                    kb.wv(ACT, sg_free[ts])
                    e_sg = kb.inc(ACT.activation(sg[ts][:], p[ts][:], AF.Silu), "act")
                    bank_free[ts] = e_sg
                    kb.wv(DVE, e_sg)
                    kb.wv(DVE, pair_evts[2 + ts])
                    kb.wv(DVE, h_free[pj])
                    e_h = kb.inc(DVE.tensor_mul(h_sb[pj][:, ts * 512:(ts + 1) * 512],
                                   sg[ts][:], p[2 + ts][:]), "dve")
                    bank_free[2 + ts] = e_h
                    sg_free[ts] = e_h
                h_evt[pj] = kb.cnt("dve")
            e_h_all = kb.cnt("dve")
            if DBG and hf == 0 and e == 0:
                kb.wv(PL, e_h_all)
                for pj6 in range(6):
                    e_dh = kb.inc(PL.dma_start(
                        dbg_h[pj6 * 128:(pj6 + 1) * 128, :], h_sb[pj6][:]),
                        "dDbg", 16)
                for pj6 in range(6):
                    h_free[pj6] = e_dh
            for tc in range(8):
                for n in range(4):
                    k2 = tc * 4 + n
                    bank = 4 + (k2 % 2)
                    kb.wv(PE, wd_load[ee])
                    kb.wv(PE, bank_free[bank])
                    for kd in range(6):
                        kb.wv(PE, h_evt[kd])
                        mm = PE.matmul(
                            p[bank][:],
                            h_sb[kd][:, tc * 128:(tc + 1) * 128],
                            wdbuf[ee % 2][:, kd * 2048 + n * 512:
                                          kd * 2048 + (n + 1) * 512],
                            start=(kd == 0), stop=(kd == 5))
                    e_z = kb.inc(mm, "pe")
                    kb.wv(DVE, e_z)
                    kb.wv(DVE, e_yl)
                    e_y = kb.inc(DVE.scalar_tensor_tensor(
                        y_T[tc][:, n * 512:(n + 1) * 512], p[bank][:],
                        cw[hf * 8 + tc][:, e:e + 1],
                        y_T[tc][:, n * 512:(n + 1) * 512],
                        ALU.mult, ALU.add), "dve")
                    if DBG and hf == 0 and e == 0 and tc == 0 and n == 3:
                        kb.wv(PL, e_y)
                        kb.inc(PL.dma_start(dbg_y0[:, :], y_T[0][:]),
                               "dDbg", 16)
                    bank_free[bank] = e_y
                    if e == NE - 1:
                        y_last[tc] = e_y
            wd_done[ee] = kb.cnt("pe")
            b2_wd_load(ee + 2)
            for pj in range(6):
                h_free[pj] = wd_done[ee]
        for tb in range(8):
            kb.wv(SP, y_last[tb])
            y_store[hf][tb] = kb.inc(
                SP.dma_start(out_h[hf * 1024 + tb * 128:
                                   hf * 1024 + (tb + 1) * 128, :], y_T[tb][:]),
                "dOut", 16)

    if DBG:
        kb.inc(SP.dma_start(dbg_x1[:, :], x1s[:, :]), "dDbg", 16)
        e_db = kb.inc(PL.dma_start(
            dbg_ot.rearrange("(a p) q -> a p q", p=128),
            ots.rearrange("(a p) q -> a p q", p=128)), "dDbg", 16)
        kb.wait(SP, "dDbg", kb.cnt("dDbg")[1])
    kb.wait(SP, "dOut", kb.cnt("dOut")[1])
    kb.wait(PL, "dOut", kb.cnt("dOut")[1])
    bctx.close()
    bkctx.close()
    akctx.close()
    pctx.close()
    top.close()
    nc.finalize()
    return nc


def kb_act_evt(e):
    return e


def kb_ds_evt(e):
    return e


def kb_dve_evt(e):
    return e


def _prep(inputs):
    f = np.float32
    x = np.ascontiguousarray(np.asarray(inputs["L0_attn_hidden_states"], f))
    wqd = np.asarray(inputs["L0_attn_Wq_down"], f)
    wqu = np.asarray(inputs["L0_attn_Wq_up"], f)
    wkvd = np.asarray(inputs["L0_attn_Wkv_down"], f)
    kup = np.asarray(inputs["L0_attn_k_up"], f)
    vup = np.asarray(inputs["L0_attn_v_up"], f)
    wo = np.asarray(inputs["L0_attn_Wo"], f)
    gw = np.asarray(inputs["L0_ffn_gate_weight"], f)
    gup = np.asarray(inputs["L0_ffn_gate_up_proj"], f)
    dwn = np.asarray(inputs["L0_ffn_down_proj"], f)
    sim = np.ascontiguousarray(np.asarray(inputs["L0_ffn_similarity"], f))

    shared = {
        "wqdT": np.ascontiguousarray(
            wqd.T.reshape(16, 128, 12, 128).transpose(2, 1, 0, 3)),
        "wquT": np.ascontiguousarray(
            wqu.T.reshape(12, 128, 16, 128).transpose(2, 1, 0, 3)),
        "wkvdT": np.ascontiguousarray(
            wkvd.T.reshape(16, 128, 4, 128).transpose(2, 1, 0, 3)),
        "kupT": np.ascontiguousarray(
            kup.transpose(1, 0, 2).reshape(512, 2048).reshape(4, 128, 2048)),
        "vupT": np.ascontiguousarray(
            vup.transpose(1, 0, 2).reshape(512, 2048).reshape(4, 128, 2048)),
        "woT": np.ascontiguousarray(wo.T).reshape(16, 128, 2048),
        "gwT": np.ascontiguousarray(
            gw.T.reshape(16, 128, 32).transpose(1, 0, 2).reshape(128, 512)),
        "sim": sim,
        "wg": np.ascontiguousarray(
            gup.transpose(0, 2, 1).reshape(32, 16, 128, 12, 128)
            .transpose(0, 3, 2, 1, 4)),
        "wd": np.ascontiguousarray(dwn.transpose(0, 2, 1)).reshape(32, 6, 128, 2048),
    }
    return x, shared


def kernel(**inputs):
    x, shared = _prep(inputs)
    nc = build()
    in_maps = []
    for core in range(8):
        m = {"x": np.ascontiguousarray(x[core])}
        m.update(shared)
        in_maps.append(m)
    res = run_bass_kernel_spmd(nc, in_maps, core_ids=list(range(8)))
    return np.stack([r["out"] for r in res.results], axis=0)
